# version 09
# baseline (speedup 1.0000x reference)
import numpy as np
from contextlib import ExitStack

import concourse.bass as bass
import concourse.mybir as mybir
from concourse.bass_utils import run_bass_kernel_spmd

F32 = mybir.dt.float32
BF16 = mybir.dt.bfloat16
AF = mybir.ActivationFunctionType
ALU = mybir.AluOpType
AX = mybir.AxisListType

P = 128
D = 1024
H = 16
HD = 64
NT = 16
HALF = 8
SB = 128
LTAB = 8
RB = 16
E = 16
ES = 256
HFF = 256
EPS = 1e-5

DBG_NAMES = ()
A_NQ = 8
A_CFG = dict(bs=(0, 1), halves=(0, 1), tiles=8, buckets=True, stages=99, sub9=99)


class Sm:
    def __init__(self, es, nc, name):
        self.h = es.enter_context(nc.semaphore(name))
        self.n = 0

    def inc(self, instr, k=1):
        instr.then_inc(self.h, k)
        self.n += k
        return self.n


def build_program(dbg_names=(), phases=("A", "B", "F")):
    nc = bass.Bass("TRN2", target_bir_lowering=False, detect_race_conditions=False)
    dbg_outs = {}

    x_d = nc.dram_tensor("x", [2, 2048, D], F32, kind="ExternalInput").ap()
    wq_d = nc.dram_tensor("wq", [128, 8192], F32, kind="ExternalInput").ap()
    wk_d = nc.dram_tensor("wk", [128, 8192], F32, kind="ExternalInput").ap()
    wv_d = nc.dram_tensor("wv", [128, 8192], F32, kind="ExternalInput").ap()
    wo_d = nc.dram_tensor("wo", [128, 8192], F32, kind="ExternalInput").ap()
    bo_d = nc.dram_tensor("bo", [1, D], F32, kind="ExternalInput").ap()
    planes_d = nc.dram_tensor("planes2", [128, 32], F32, kind="ExternalInput").ap()
    protos_d = nc.dram_tensor("protos", [4, 16], F32, kind="ExternalInput").ap()
    ltemp_d = nc.dram_tensor("ltemp", [1, 1], F32, kind="ExternalInput").ap()
    negA_d = nc.dram_tensor("negindA", [64, 1024], F32, kind="ExternalInput").ap()
    negB_d = nc.dram_tensor("negindB", [8, 1024], F32, kind="ExternalInput").ap()
    ident_d = nc.dram_tensor("ident", [128, 128], F32, kind="ExternalInput").ap()
    ones1_d = nc.dram_tensor("ones1", [1, 128], F32, kind="ExternalInput").ap()
    onescol_d = nc.dram_tensor("onescol", [128, 1], F32, kind="ExternalInput").ap()
    nh8_d = nc.dram_tensor("nh8", [128, 8], F32, kind="ExternalInput").ap()
    nh1_d = nc.dram_tensor("nh1", [128, 1], F32, kind="ExternalInput").ap()
    ph1_d = nc.dram_tensor("ph1", [128, 1], F32, kind="ExternalInput").ap()
    mu_d = nc.dram_tensor("mul", [128, 2048], F32, kind="ExternalInput").ap()
    gam_d = nc.dram_tensor("gam", [128, D], F32, kind="ExternalInput").ap()
    bet_d = nc.dram_tensor("bet", [128, D], F32, kind="ExternalInput").ap()
    scl_d = nc.dram_tensor("scl", [128, 1], F32, kind="ExternalInput").ap()
    w1_d = nc.dram_tensor("w1l", [E, 128, 2048], F32, kind="ExternalInput").ap()
    w2_d = nc.dram_tensor("w2l", [E, 128, 2048], F32, kind="ExternalInput").ap()
    b1_d = nc.dram_tensor("b1f", [128, 32], F32, kind="ExternalInput").ap()
    b2_d = nc.dram_tensor("b2r", [256, D], F32, kind="ExternalInput").ap()

    out_d = nc.dram_tensor("out", [2, 2048, D], F32, kind="ExternalOutput").ap()
    x2scr = nc.dram_tensor("x2scr", [2, 2048, D], F32, kind="Internal").ap()

    x_half_v = x_d.rearrange("b (hh t p) d -> b hh p t d", hh=2, t=HALF, p=P)
    x_tile_v = x_d.rearrange("b (t p) d -> b t p d", t=NT, p=P)
    x2_tile_v = x2scr.rearrange("b (t p) d -> b t p d", t=NT, p=P)
    out_tile_v = out_d.rearrange("b (t p) d -> b t p d", t=NT, p=P)

    es = ExitStack()
    with es:
        sP = Sm(es, nc, "sP")
        sV = Sm(es, nc, "sV")
        sA = Sm(es, nc, "sA")
        sG = Sm(es, nc, "sG")
        dXA = Sm(es, nc, "dXA")
        dXA2 = Sm(es, nc, "dXA2")
        dXp = [Sm(es, nc, "dX0"), Sm(es, nc, "dX1")]
        dCon = Sm(es, nc, "dCon")
        dPro = Sm(es, nc, "dPro")
        dWkv = Sm(es, nc, "dWkv")
        dWqo = Sm(es, nc, "dWqo")
        dE = [Sm(es, nc, f"dE{i}") for i in range(4)]
        dSp = [Sm(es, nc, "dS0"), Sm(es, nc, "dS1")]
        dSh = [Sm(es, nc, "dSh0"), Sm(es, nc, "dSh1")]
        dO = Sm(es, nc, "dO")
        dOp = [Sm(es, nc, "dO0"), Sm(es, nc, "dO1")]
        dDbg = Sm(es, nc, "dDbg")

        ptrb_t = es.enter_context(nc.psum_tensor("ptrb", [P, 2048], BF16))
        ptrb = ptrb_t.ap()[:, 0:1024]
        ptrbB = ptrb_t.ap()[:, 1024:2048]
        pA = es.enter_context(nc.psum_tensor("pA", [P, 512], F32))
        pB = es.enter_context(nc.psum_tensor("pB", [P, 512], F32))
        pC = es.enter_context(nc.psum_tensor("pC", [P, 512], F32))
        pD = es.enter_context(nc.psum_tensor("pD", [P, 512], F32))
        pE = es.enter_context(nc.psum_tensor("pE", [P, 1024], F32))

        identb = es.enter_context(nc.sbuf_tensor("identb", [128, 128], BF16))
        identf = es.enter_context(nc.sbuf_tensor("identf", [128, 128], F32))
        onesb = es.enter_context(nc.sbuf_tensor("onesb", [1, 128], BF16))
        ones1f = es.enter_context(nc.sbuf_tensor("ones1f", [1, 128], F32))
        onescolb = es.enter_context(nc.sbuf_tensor("onescolb", [128, 1], BF16))
        negA = es.enter_context(nc.sbuf_tensor("negA", [64, 1024], BF16))
        negB = es.enter_context(nc.sbuf_tensor("negB", [8, 1024], BF16))
        planes = es.enter_context(nc.sbuf_tensor("planes", [128, 32], BF16))
        PbigF = es.enter_context(nc.sbuf_tensor("PbigF", [32, 128], F32))
        PbigS = es.enter_context(nc.sbuf_tensor("PbigS", [128, 128], BF16))
        bo_s = es.enter_context(nc.sbuf_tensor("bo_s", [1, D], BF16))
        lt_s = es.enter_context(nc.sbuf_tensor("lt_s", [1, 8], F32))
        nh8_s = es.enter_context(nc.sbuf_tensor("nh8_s", [128, 8], F32))
        nh1_s = es.enter_context(nc.sbuf_tensor("nh1_s", [128, 1], F32))
        ph1_s = es.enter_context(nc.sbuf_tensor("ph1_s", [128, 1], F32))
        invs32 = es.enter_context(nc.sbuf_tensor("invs32", [32, 1], F32))

        block = es.enter_context(nc.Block())
        T = block.tensor
        V = block.vector
        A = block.scalar
        G = block.gpsimd
        Q = block.sync

        def dump(name, ap, sem, val):
            if name not in dbg_names:
                return
            dd = nc.dram_tensor("dbg_" + name, list(ap.shape), F32, kind="ExternalOutput").ap()
            dbg_outs[name] = dd

            def _(g):
                g.wait_ge(sem.h, val)
                dDbg.inc(g.dma_start(out=dd, in_=ap), 16)
            G(_)

        def _(g):
            g.dma_start(out=identb[:], in_=ident_d).then_inc(dCon.h, 16)
            g.dma_start(out=identf[:], in_=ident_d).then_inc(dCon.h, 16)
            g.dma_start(out=onesb[:], in_=ones1_d).then_inc(dCon.h, 16)
            g.dma_start(out=ones1f[:], in_=ones1_d).then_inc(dCon.h, 16)
            g.dma_start(out=onescolb[:], in_=onescol_d).then_inc(dCon.h, 16)
            g.dma_start(out=negA[:], in_=negA_d).then_inc(dCon.h, 16)
            g.dma_start(out=negB[:], in_=negB_d).then_inc(dCon.h, 16)
            g.dma_start(out=planes[:], in_=planes_d).then_inc(dCon.h, 16)
            g.dma_start(out=bo_s[:], in_=bo_d).then_inc(dCon.h, 16)
            g.dma_start(out=lt_s[0:1, 0:1], in_=ltemp_d).then_inc(dCon.h, 16)
            g.dma_start(out=nh8_s[:], in_=nh8_d).then_inc(dCon.h, 16)
            g.dma_start(out=nh1_s[:], in_=nh1_d).then_inc(dCon.h, 16)
            g.dma_start(out=ph1_s[:], in_=ph1_d).then_inc(dCon.h, 16)
            dCon.n += 208
        G(_)
        w_pre = dCon.n

        def _(v):
            sV.inc(v.memset(PbigF[:], 0.0))
        V(_)
        v_pz = sV.n

        def _(g):
            g.wait_ge(sV.h, v_pz)
            for l in range(LTAB):
                g.dma_start(out=PbigF[4 * l:4 * l + 4, 16 * l:16 * l + 16], in_=protos_d).then_inc(dPro.h, 16)
            dPro.n += 16 * LTAB
        G(_)
        w_pbig = dPro.n

        def _(s):
            s.wait_ge(dCon.h, w_pre)
            sA.inc(s.activation(lt_s[0:1, 1:2], lt_s[0:1, 0:1], AF.Exp))
        A(_)
        a_e = sA.n

        def _(v):
            v.wait_ge(sA.h, a_e)
            sV.inc(v.tensor_scalar(lt_s[0:1, 2:3], lt_s[0:1, 1:2], 20.0, 0.01, op0=ALU.min, op1=ALU.max))
            v.wait_ge(sV.h, sV.n)
            sV.inc(v.reciprocal(lt_s[0:1, 3:4], lt_s[0:1, 2:3]))
        V(_)
        v_invs = sV.n

        def _(t):
            t.wait_ge(sV.h, v_invs)
            sP.inc(t.matmul(pD[0:32, 0:1], ones1f[0:1, 0:32], lt_s[0:1, 3:4], start=True, stop=True))
        T(_)
        p_inv = sP.n

        def _(s):
            s.wait_ge(sP.h, p_inv)
            sA.inc(s.copy(invs32[:], pD[0:32, 0:1]))
        A(_)
        a_inv = sA.n

        def _(v):
            v.wait_ge(sA.h, a_inv)
            v.wait_ge(dPro.h, w_pbig)
            sV.inc(v.tensor_scalar(PbigS[0:32, :], PbigF[:], invs32[:], None, op0=ALU.mult))
            sV.inc(v.tensor_scalar(PbigS[64:96, :], PbigF[:], invs32[:], None, op0=ALU.mult))
        V(_)
        v_pbig = sV.n
        dump("PbigS", PbigS[0:32, :], sV, v_pbig)


        stats = es.enter_context(nc.sbuf_tensor("stats", [P, 64], F32))
        negm1 = es.enter_context(nc.sbuf_tensor("negm1", [P, 32], F32))
        r1s = es.enter_context(nc.sbuf_tensor("r1s", [P, 32], F32))

        esAB = es.enter_context(ExitStack())
        bKTa = [esAB.enter_context(nc.sbuf_tensor(f"bKTa{b}", [64, 2048], F32)) for b in range(2)]
        bVa = [esAB.enter_context(nc.sbuf_tensor(f"bVa{b}", [P, 1024], F32)) for b in range(2)]

        esA = ExitStack()
        wk_s = esA.enter_context(nc.sbuf_tensor("wk_s", [128, 8192], BF16))
        wv_s = esA.enter_context(nc.sbuf_tensor("wv_s", [128, 8192], BF16))
        x_sb = esA.enter_context(nc.sbuf_tensor("x_sb", [P, HALF, D], F32))
        xn1p = [esA.enter_context(nc.sbuf_tensor(f"xn1p{i}", [P, D], BF16)) for i in range(2)]
        xn1f_q = esA.enter_context(nc.sbuf_tensor("xn1f_q", [P, 4, D], BF16))
        Kfm_q = esA.enter_context(nc.sbuf_tensor("Kfm_q", [P, 8, 512], BF16))
        tanhp_q = esA.enter_context(nc.sbuf_tensor("tanhp_q", [P, 4, D], BF16))
        Ktm_qp = [esA.enter_context(nc.sbuf_tensor(f"Ktm_qp{i}", [P, 4, D], BF16)) for i in range(2)]
        Vtm_qp = [esA.enter_context(nc.sbuf_tensor(f"Vtm_qp{i}", [P, 4, D], BF16)) for i in range(2)]
        probs_qp = [esA.enter_context(nc.sbuf_tensor(f"probs_qp{i}", [P, 4, 2048], BF16)) for i in range(2)]
        bnbuf = esA.enter_context(nc.sbuf_tensor("bnbuf", [P, 96], F32))
        tsb = esA.enter_context(nc.sbuf_tensor("tsb", [P, 8], F32))
        nm64 = [esA.enter_context(nc.sbuf_tensor(f"nm64_{i}", [P, 64], F32)) for i in range(2)]
        Z64 = [esA.enter_context(nc.sbuf_tensor(f"Z64_{i}", [P, 64], F32)) for i in range(2)]
        R64 = [esA.enter_context(nc.sbuf_tensor(f"R64_{i}", [P, 64], F32)) for i in range(2)]
        R64b = [esA.enter_context(nc.sbuf_tensor(f"R64b_{i}", [P, 64], BF16)) for i in range(2)]
        scrA = [esA.enter_context(nc.sbuf_tensor(f"scrA{i}", [P, 1024], F32)) for i in range(2)]

        def _(g):
            g.dma_start(out=wk_s[:], in_=wk_d).then_inc(dWkv.h, 16)
            g.dma_start(out=wv_s[:], in_=wv_d).then_inc(dWkv.h, 16)
            dWkv.n += 32
        G(_)
        w_attn = dWkv.n

        trk = {
            "x_free": 0,
            "ptrb_free": 0,
            "ptrf_free": 0,
            "pAB_free_pe": 0,
            "pCD_free": 0,
            "pE_free": 0,
            "prod_free_pe": 0,
            "bk_ready": 0,
            "x2_store": 0,
        }

        dXq = [dXA, dXA2]
        atrk = {
            "pt_free": [0, 0],
            "xp_free": [0, 0],
            "xldv": {},
            "r1s": {},
            "xe": {},
            "xn": {},
            "ke": {},
            "kte": {},
            "th": {},
            "vm": {},
            "ve": {},
            "lg": {},
            "ex": {},
            "sub": {},
            "norm": {},
            "bk_adds": {},
        }
        SCB_A = [(pE[:, 0:512], pE[:, 512:1024]), (pA[:, 0:512], pB[:, 0:512])]

        def a_load_half(it_next):
            b, half = it_next // 4, (it_next % 4) // 2
            g1 = atrk["xn"].get(it_next - 2, 0)
            g2 = atrk["xn"].get(it_next - 1, 0)

            def _(q, b=b, half=half, g1=g1, g2=g2):
                if g1:
                    q.wait_ge(sV.h, g1)
                dXq[0].inc(q.dma_start(out=x_sb[:, 0:4, :], in_=x_half_v[b, half][:, 0:4, :]), 16)
                if g2:
                    q.wait_ge(sV.h, g2)
                dXq[1].inc(q.dma_start(out=x_sb[:, 4:8, :], in_=x_half_v[b, half][:, 4:8, :]), 16)
            Q(_)
            atrk["xldv"][it_next] = dXq[0].n
            atrk["xldv"][it_next + 1] = dXq[1].n

        def a_stats_q(it):
            b, qq = it // 4, it % 4
            sgate = atrk["xldv"][it]
            dX = dXq[it % 2]
            c0 = (qq % 2) * 4

            def _(v, b=b, qq=qq, c0=c0, sgate=sgate, dX=dX):
                v.wait_ge(dX.h, sgate)
                for tl in range(4):
                    o = (c0 + tl) * 12
                    sV.inc(v.bn_stats(bnbuf[:, o:o + 6], x_sb[:, c0 + tl, 0:512]))
                    sV.inc(v.bn_stats(bnbuf[:, o + 6:o + 12], x_sb[:, c0 + tl, 512:1024]))
                v.wait_ge(sV.h, sV.n)
                for tl in range(4):
                    col = b * 32 + (qq * 4 + tl) * 2
                    o = (c0 + tl) * 12
                    sV.inc(v.bn_aggr(stats[:, col:col + 2], bnbuf[:, o:o + 12]))
                v.wait_ge(sV.h, sV.n)
                t0 = b * 16 + qq * 4
                vsl = stats[:, b * 32 + qq * 8 + 1: b * 32 + qq * 8 + 8: 2]
                msl = stats[:, b * 32 + qq * 8: b * 32 + qq * 8 + 8: 2]
                sV.inc(v.tensor_scalar(tsb[:, c0:c0 + 4], vsl, EPS, None, op0=ALU.add))
                sV.inc(v.tensor_scalar(negm1[:, t0:t0 + 4], msl, -1.0, None, op0=ALU.mult))
            V(_)
            v_stats = sV.n

            def _(g, b=b, qq=qq, c0=c0, v_stats=v_stats):
                g.wait_ge(sV.h, v_stats)
                g.wait_ge(dCon.h, 208)
                t0 = b * 16 + qq * 4
                sG.inc(g.tensor_tensor(r1s[:, t0:t0 + 4], tsb[:, c0:c0 + 4], nh8_s[:, 0:4], op=ALU.pow))
            G(_)
            atrk["r1s"][it] = sG.n

        def a_xnxt(it):
            b, qq = it // 4, it % 4
            r1s_g = atrk["r1s"][it]
            x_ld = atrk["xldv"][it]
            dX = dXq[it % 2]
            vm_prev = atrk["vm"].get(it - 1, 0)
            for tl in range(4):
                par = tl % 2
                ncol = b * 16 + qq * 4 + tl
                hh_tl = (qq % 2) * 4 + tl
                xp_free = atrk["xp_free"][par]

                def _(v, ncol=ncol, hh_tl=hh_tl, par=par, r1s_g=r1s_g, x_ld=x_ld, dX=dX, xp_free=xp_free):
                    v.wait_ge(dX.h, x_ld)
                    v.wait_ge(sG.h, r1s_g)
                    if xp_free:
                        v.wait_ge(sP.h, xp_free)
                    sV.inc(v.tensor_scalar(xn1p[par][:], x_sb[:, hh_tl, :], negm1[:, ncol:ncol + 1],
                                           r1s[:, ncol:ncol + 1], op0=ALU.add, op1=ALU.mult))
                V(_)
                v_xn = sV.n

                pt = ptrb if par == 0 else ptrbB
                pt_free = atrk["pt_free"][par]

                def _(t, it=it, pt=pt, par=par, v_xn=v_xn, pt_free=pt_free):
                    t.wait_ge(sV.h, v_xn)
                    if it == 0:
                        t.wait_ge(dCon.h, 208)
                    if pt_free:
                        t.wait_ge(sA.h, pt_free)
                    for kc in range(8):
                        ii = t.transpose(pt[:, kc * 128:kc * 128 + 128], xn1p[par][:, kc * 128:kc * 128 + 128], identb[:])
                    sP.inc(ii)
                T(_)
                atrk["xp_free"][par] = sP.n
                p_xt = sP.n

                def _(s, tl=tl, pt=pt, p_xt=p_xt, vm_prev=vm_prev):
                    s.wait_ge(sP.h, p_xt)
                    if vm_prev:
                        s.wait_ge(sP.h, vm_prev)
                    sA.inc(s.copy(xn1f_q[:, tl, :], pt[:]))
                A(_)
                atrk["pt_free"][par] = sA.n
            atrk["xe"][it] = sA.n
            atrk["xn"][it] = sV.n

        def a_km(it, oc):
            bank = pC if oc % 2 == 0 else pD
            if oc < 2:
                w_free = atrk["ve"].get(it - 1, 0)
            else:
                w_free = atrk["ke"][(it, oc - 2)]
            xe_v = atrk["xe"][it]

            def _(t, it=it, oc=oc, bank=bank, w_free=w_free, xe_v=xe_v):
                t.wait_ge(sA.h, xe_v)
                if it == 0 and oc == 0:
                    t.wait_ge(dWkv.h, 32)
                if w_free:
                    t.wait_ge(sA.h, w_free)
                for kc in range(8):
                    ii = t.matmul(bank[:, 0:512],
                                  wk_s[:, kc * 1024 + oc * 128:kc * 1024 + oc * 128 + 128],
                                  xn1f_q[:, 0:4, kc * 128:kc * 128 + 128],
                                  start=(kc == 0), stop=(kc == 7), skip_group_check=True)
                sP.inc(ii)
            T(_)
            p_km = sP.n

            def _(s, oc=oc, bank=bank, p_km=p_km):
                s.wait_ge(sP.h, p_km)
                sA.inc(s.copy(Kfm_q[:, oc, :], bank[:]))
            A(_)
            atrk["ke"][(it, oc)] = sA.n

        def a_kt(it):
            ke_last = atrk["ke"][(it, 7)]
            pq = it % 2
            for tl in range(4):
                par = tl % 2
                pt = ptrb if par == 0 else ptrbB
                pt_free = atrk["pt_free"][par]

                def _(t, tl=tl, pt=pt, ke_last=ke_last, pt_free=pt_free):
                    t.wait_ge(sA.h, ke_last)
                    if pt_free:
                        t.wait_ge(sA.h, pt_free)
                    for oc in range(8):
                        ii = t.transpose(pt[:, oc * 128:oc * 128 + 128], Kfm_q[:, oc, tl * 128:tl * 128 + 128], identb[:])
                    sP.inc(ii)
                T(_)
                p_kt = sP.n

                def _(s, tl=tl, pt=pt, p_kt=p_kt, pq=pq):
                    s.wait_ge(sP.h, p_kt)
                    sA.inc(s.copy(Ktm_qp[pq][:, tl, :], pt[:]))
                A(_)
                atrk["pt_free"][par] = sA.n
            atrk["kte"][it] = sA.n

        def a_pjth(it):
            for h in range(H):
                bank = pC if h % 2 == 0 else pD
                base = 64 * (h % 2)
                oc = h // 2
                if h < 2:
                    w_free = atrk["ke"][(it, 6 + h)]
                else:
                    w_free = atrk["th"][(it, h - 2)]

                def _(t, bank=bank, base=base, oc=oc, w_free=w_free):
                    t.wait_ge(sA.h, w_free)
                    sP.inc(t.matmul(bank[base:base + 32, 0:512],
                                    planes[base:base + 64, :],
                                    Kfm_q[base:base + 64, oc, :],
                                    start=True, stop=True))
                T(_)
                p_pj = sP.n

                def _(s, bank=bank, base=base, oc=oc, p_pj=p_pj):
                    s.wait_ge(sP.h, p_pj)
                    sA.inc(s.activation(tanhp_q[base:base + 32, 0:4, oc * 128:oc * 128 + 128],
                                        bank[base:base + 32, 0:512].rearrange("p (t c) -> p t c", c=128),
                                        AF.Tanh))
                A(_)
                atrk["th"][(it, h)] = sA.n

        def a_vmve(it):
            pq = it % 2
            ve_prev = 0
            for tl in range(4):
                w_th = atrk["th"][(it, 15)] if tl == 0 else 0

                def _(t, tl=tl, w_th=w_th, ve_prev=ve_prev):
                    if w_th:
                        t.wait_ge(sA.h, w_th)
                    if ve_prev:
                        t.wait_ge(sA.h, ve_prev)
                    for dh in range(2):
                        bank = pC if dh == 0 else pD
                        for kc in range(8):
                            ii = t.matmul(bank[:, 0:512],
                                          xn1f_q[:, tl, kc * 128:kc * 128 + 128],
                                          wv_s[:, kc * 1024 + dh * 512:kc * 1024 + dh * 512 + 512],
                                          start=(kc == 0), stop=(kc == 7))
                    sP.inc(ii)
                T(_)
                p_vm = sP.n
                atrk["vm"][it] = p_vm

                def _(s, tl=tl, p_vm=p_vm, pq=pq):
                    s.wait_ge(sP.h, p_vm)
                    sA.inc(s.copy(Vtm_qp[pq][:, tl, 0:512], pC[:]))
                    sA.inc(s.copy(Vtm_qp[pq][:, tl, 512:1024], pD[:]))
                A(_)
                ve_prev = sA.n
            atrk["ve"][it] = sA.n

        def a_lg(it, j):
            tl, grp = j // 2, j % 2
            par = j % 2
            banks = SCB_A[par]
            if j >= 2:
                w_ex = atrk["sub"][(it, j - 2)]
            elif j == 0:
                w_ex = atrk["sub"].get((it - 1, 6), 0)
            else:
                w_ex = 0
            w_bk = atrk["bk_adds"].get(it - 1, 0) if j == 1 else 0
            w_th = atrk["th"][(it, 15)]
            w_pb = v_pbig if it == 0 else 0

            def _(t, tl=tl, grp=grp, banks=banks, w_ex=w_ex, w_bk=w_bk, w_th=w_th, w_pb=w_pb):
                t.wait_ge(sA.h, w_th)
                if w_ex:
                    t.wait_ge(sV.h, w_ex)
                if w_bk:
                    t.wait_ge(sV.h, w_bk)
                if w_pb:
                    t.wait_ge(sV.h, w_pb)
                for hl in range(8):
                    h = grp * 8 + hl
                    base = 64 * (h % 2)
                    bank = banks[hl % 2]
                    ii = t.matmul(bank[:, (hl // 2) * 128:(hl // 2) * 128 + 128],
                                  tanhp_q[base:base + 32, tl, (h // 2) * 128:(h // 2) * 128 + 128],
                                  PbigS[base:base + 32, :],
                                  start=(hl in (0, 1)), stop=(hl in (6, 7)),
                                  skip_group_check=True)
                sP.inc(ii)
            T(_)
            atrk["lg"][(it, j)] = sP.n

        def a_nmex(it, j):
            tl, grp = j // 2, j % 2
            par = j % 2
            banks = SCB_A[par]
            w_lg = atrk["lg"][(it, j)]

            def _(v, par=par, banks=banks, w_lg=w_lg):
                v.wait_ge(sP.h, w_lg)
                sV.inc(v.tensor_reduce(nm64[par][:, 0:32], banks[0].rearrange("p (s r) -> p s r", r=16),
                                       axis=AX.X, op=ALU.max, negate=True))
                sV.inc(v.tensor_reduce(nm64[par][:, 32:64], banks[1].rearrange("p (s r) -> p s r", r=16),
                                       axis=AX.X, op=ALU.max, negate=True))
                v.wait_ge(sV.h, sV.n)
                sV.inc(v.tensor_tensor(scrA[par][:, 0:512].rearrange("p (s r) -> p s r", r=16),
                                       banks[0].rearrange("p (s r) -> p s r", r=16),
                                       nm64[par][:, 0:32].unsqueeze(2).broadcast_to([P, 32, 16]), op=ALU.add))
                sV.inc(v.tensor_tensor(scrA[par][:, 512:1024].rearrange("p (s r) -> p s r", r=16),
                                       banks[1].rearrange("p (s r) -> p s r", r=16),
                                       nm64[par][:, 32:64].unsqueeze(2).broadcast_to([P, 32, 16]), op=ALU.add))
            V(_)
            w_sub = sV.n
            atrk["sub"][(it, j)] = w_sub

            pq = it % 2

            def _(s, tl=tl, grp=grp, par=par, w_sub=w_sub, pq=pq):
                s.wait_ge(sV.h, w_sub)
                dst = probs_qp[pq][:, tl, grp * 1024:grp * 1024 + 1024].rearrange("p (c k) -> p c k", k=256)
                sA.inc(s.activation(dst[:, :, 0:128],
                                    scrA[par][:, 0:512].rearrange("p (c s) -> p c s", s=128), AF.Exp))
                sA.inc(s.activation(dst[:, :, 128:256],
                                    scrA[par][:, 512:1024].rearrange("p (c s) -> p c s", s=128), AF.Exp))
            A(_)
            atrk["ex"][(it, j)] = sA.n

        def a_norm(it, j):
            tl, grp = j // 2, j % 2
            par = j % 2
            pq = it % 2
            w_ex = atrk["ex"][(it, j)]

            def _(v, tl=tl, grp=grp, par=par, pq=pq, w_ex=w_ex):
                v.wait_ge(sA.h, w_ex)
                sl = probs_qp[pq][:, tl, grp * 1024:grp * 1024 + 1024].rearrange("p (s r) -> p s r", r=16)
                sV.inc(v.tensor_reduce(Z64[par][:], sl, axis=AX.X, op=ALU.add))
                v.wait_ge(sV.h, sV.n)
                sV.inc(v.reciprocal(R64[par][:], Z64[par][:]))
                v.wait_ge(sV.h, sV.n)
                sV.inc(v.tensor_copy(R64b[par][:], R64[par][:]))
                v.wait_ge(sV.h, sV.n)
                sV.inc(v.tensor_tensor(sl, sl, R64b[par][:].unsqueeze(2).broadcast_to([P, 64, 16]), op=ALU.mult))
            V(_)
            atrk["norm"][(it, j)] = sV.n

        def a_bk(it):
            b, qq = it // 4, it % 4
            pq = it % 2
            w_norm = atrk["norm"][(it, 7)]
            w_kte = atrk["kte"][it]
            w_ve = atrk["ve"][it]
            last_add = [0, 0]
            for h in range(H):
                bank = pA if h % 2 == 0 else pB
                prev_add = last_add[h % 2]

                def _(t, h=h, bank=bank, pq=pq, w_norm=w_norm, w_kte=w_kte, w_ve=w_ve, prev_add=prev_add):
                    if h < 2:
                        t.wait_ge(sV.h, w_norm)
                        t.wait_ge(sA.h, w_kte)
                        t.wait_ge(sA.h, w_ve)
                    if prev_add:
                        t.wait_ge(sV.h, prev_add)
                    for tt in range(4):
                        t.matmul(bank[0:64, 0:128], Ktm_qp[pq][:, tt, h * 64:h * 64 + 64],
                                 probs_qp[pq][:, tt, h * 128:h * 128 + 128],
                                 start=(tt == 0), stop=(tt == 3), skip_group_check=True)
                    for tt in range(4):
                        ii = t.matmul(bank[0:128, 128:192], probs_qp[pq][:, tt, h * 128:h * 128 + 128],
                                      Vtm_qp[pq][:, tt, h * 64:h * 64 + 64],
                                      start=(tt == 0), stop=(tt == 3), skip_group_check=True)
                    sP.inc(ii)
                T(_)
                p_bk = sP.n

                def _(v, b=b, qq=qq, h=h, bank=bank, p_bk=p_bk):
                    v.wait_ge(sP.h, p_bk)
                    if qq == 0:
                        sV.inc(v.tensor_copy(bKTa[b][:, h * 128:h * 128 + 128], bank[0:64, 0:128]))
                        sV.inc(v.tensor_copy(bVa[b][:, h * 64:h * 64 + 64], bank[0:128, 128:192]))
                    else:
                        sV.inc(v.tensor_tensor(bKTa[b][:, h * 128:h * 128 + 128], bKTa[b][:, h * 128:h * 128 + 128],
                                               bank[0:64, 0:128], op=ALU.add))
                        sV.inc(v.tensor_tensor(bVa[b][:, h * 64:h * 64 + 64], bVa[b][:, h * 64:h * 64 + 64],
                                               bank[0:128, 128:192], op=ALU.add))
                V(_)
                last_add[h % 2] = sV.n
            atrk["bk_adds"][it] = sV.n

        def emit_A_all():
            NQ = A_NQ
            a_load_half(0)
            a_stats_q(0)
            a_xnxt(0)
            for it in range(NQ):
                for j in range(10):
                    if j < 8:
                        a_km(it, j)
                    if it >= 1:
                        if j < 8:
                            a_lg(it - 1, j)
                        if 1 <= j <= 8:
                            a_nmex(it - 1, j - 1)
                        if 2 <= j <= 9:
                            a_norm(it - 1, j - 2)
                a_kt(it)
                a_pjth(it)
                a_vmve(it)
                if it + 1 < NQ:
                    if (it + 1) % 2 == 0:
                        a_load_half(it + 1)
                    a_stats_q(it + 1)
                    a_xnxt(it + 1)
                if it >= 1:
                    a_bk(it - 1)
                    if it - 1 == 3:
                        trk["bk_evac0"] = atrk["bk_adds"][3]
            for j in range(10):
                if j < 8:
                    a_lg(NQ - 1, j)
                if 1 <= j <= 8:
                    a_nmex(NQ - 1, j - 1)
                if 2 <= j <= 9:
                    a_norm(NQ - 1, j - 2)
            a_bk(NQ - 1)
            trk["bk_evac1"] = atrk["bk_adds"][NQ - 1]
            trk.setdefault("bk_evac0", trk["bk_evac1"])
            dump("bKT_b0", bKTa[0][:], sV, trk["bk_evac0"])
            dump("bV_b0", bVa[0][:], sV, trk["bk_evac0"])
            dump("bKT_b1", bKTa[1][:], sV, trk["bk_evac1"])
            dump("bV_b1", bVa[1][:], sV, trk["bk_evac1"])
            dump("xn1f_q0", xn1f_q[:], sA, atrk["xe"][0])
            dump("Kfm_q0", Kfm_q[:], sA, atrk["ke"][(0, 7)])
            dump("Ktm_q0", Ktm_qp[0][:], sA, atrk["kte"][0])
            dump("Vtm_q0", Vtm_qp[0][:], sA, atrk["ve"][0])
            dump("tanhp_q0a", tanhp_q[0:32, :, :], sA, atrk["th"][(0, 15)])
            dump("tanhp_q0b", tanhp_q[64:96, :, :], sA, atrk["th"][(0, 15)])
            dump("probs_q0", probs_qp[0][:], sV, atrk["norm"][(0, 7)])

        def emit_B(b):
            def _(v, b=b):
                v.wait_ge(sV.h, trk[f"bk_evac{b}"])
                sV.inc(v.tensor_copy(bKTb[b][0:64, :], bKTa[b][:]))
                sV.inc(v.tensor_copy(bKTb[b][64:128, :], bKTa[b][:]))
                sV.inc(v.tensor_copy(bVb[b][:], bVa[b][:]))
            V(_)
            bk_ready = sV.n

            for tt in range(NT):
                par = tt % 2
                xin_free = trk.get(f"xinfree_{par}", 0)

                def _(q, b=b, tt=tt, par=par, xin_free=xin_free):
                    if xin_free:
                        q.wait_ge(sV.h, xin_free)
                    dXp[par].inc(q.dma_start(out=xin[par][:], in_=x_tile_v[b, tt]), 16)
                Q(_)
                x_ld = dXp[par].n

                p_trs_prev = trk.get(f"b0trs_{par}", 0)

                def _(v, b=b, tt=tt, par=par, x_ld=x_ld, p_trs_prev=p_trs_prev):
                    v.wait_ge(dXp[par].h, x_ld)
                    if p_trs_prev:
                        v.wait_ge(sP.h, p_trs_prev)
                    ncol = b * 16 + tt
                    sV.inc(v.tensor_scalar(xnB[par][:], xin[par][:], negm1[:, ncol:ncol + 1],
                                           r1s[:, ncol:ncol + 1], op0=ALU.add, op1=ALU.mult))
                V(_)
                v_xn1 = sV.n
                trk[f"xinfree_{par}"] = v_xn1

                pt = ptrb if par == 0 else ptrbB
                a_evac_prev = trk.get(f"b0ev_{par}", 0)

                def _(t, par=par, pt=pt, v_xn1=v_xn1, a_evac_prev=a_evac_prev):
                    t.wait_ge(sV.h, v_xn1)
                    if a_evac_prev:
                        t.wait_ge(sA.h, a_evac_prev)
                    for kc in range(8):
                        ii = t.transpose(pt[:, kc * 128:kc * 128 + 128], xnB[par][:, kc * 128:kc * 128 + 128], identb[:])
                    sP.inc(ii)
                T(_)
                trk[f"b0trs_{par}"] = sP.n
                p_trs = sP.n

                def _(s, tt=tt, pt=pt, p_trs=p_trs):
                    s.wait_ge(sP.h, p_trs)
                    sA.inc(s.copy(xn1f_all[:, tt, :], pt[:]))
                A(_)
                trk[f"b0ev_{par}"] = sA.n
            a_xn1f_all = sA.n

            qb = [pA, pB, pC, pD]
            prev_evac = [0, 0, 0, 0]
            for oc in range(8):
                def _(t, oc=oc, a_xn1f_all=a_xn1f_all, pe=tuple(prev_evac)):
                    t.wait_ge(sA.h, a_xn1f_all)
                    t.wait_ge(dWqo.h, 32)
                    for tc in range(4):
                        if pe[tc]:
                            t.wait_ge(sA.h, pe[tc])
                    for kc in range(8):
                        for tc in range(4):
                            ii = t.matmul(qb[tc][:, 0:512],
                                          wq_s[:, kc * 1024 + oc * 128:kc * 1024 + oc * 128 + 128],
                                          xn1f_all[:, tc * 4:tc * 4 + 4, kc * 128:kc * 128 + 128],
                                          start=(kc == 0), stop=(kc == 7), skip_group_check=True)
                    sP.inc(ii)
                T(_)
                p_q = sP.n

                for tc in range(4):
                    def _(s, oc=oc, tc=tc, p_q=p_q):
                        s.wait_ge(sP.h, p_q)
                        sA.inc(s.copy(Q_all[:, tc * 4:tc * 4 + 4, oc * 128:oc * 128 + 128],
                                      qb[tc][:, 0:512].rearrange("p (t c) -> p t c", c=128)))
                    A(_)
                    prev_evac[tc] = sA.n
            a_q_all = sA.n

            SCB = [(pE[:, 0:512], pE[:, 512:1024]), (pA[:, 0:512], pB[:, 0:512])]
            POB = [pC, pD]
            PTB = [ptrb, ptrbB]
            nsl = NT * 2
            st_sc = [0] * nsl
            st_ex = [0] * nsl
            st_at = [0] * nsl
            st_ae = [0] * nsl
            st_oe = [0] * nsl
            st_nm = [0] * nsl
            st_ct = [0] * nsl
            st_bm = [0] * nsl

            def SC(i):
                tt, grp = divmod(i, 2)
                par = i % 2
                banks = SCB[par]
                w_ex = st_ex[i - 2] if i >= 2 else 0
                w_oe = st_oe[i - 2] if i >= 2 else 0

                def _(t, b=b, tt=tt, grp=grp, par=par, banks=banks, w_ex=w_ex, w_oe=w_oe,
                      a_q_all=a_q_all, bk_ready=bk_ready):
                    t.wait_ge(sA.h, a_q_all)
                    t.wait_ge(sV.h, bk_ready)
                    if w_ex:
                        t.wait_ge(sA.h, w_ex)
                    if w_oe:
                        t.wait_ge(sA.h, w_oe)
                    for hl in range(8):
                        h = grp * 8 + hl
                        base = 64 * (h % 2)
                        bank = banks[hl % 2]
                        ii = t.matmul(bank[:, (hl // 2) * 128:(hl // 2) * 128 + 128],
                                      Q_all[base:base + 64, tt, (h // 2) * 128:(h // 2) * 128 + 128],
                                      bKTb[b][base:base + 64, h * 128:h * 128 + 128],
                                      start=(hl in (0, 1)), stop=(hl in (6, 7)),
                                      skip_group_check=True)
                    sP.inc(ii)
                T(_)
                st_sc[i] = sP.n

            def NM(i):
                par = i % 2
                banks = SCB[par]

                def _(v, par=par, banks=banks, w=st_sc[i]):
                    v.wait_ge(sP.h, w)
                    v.tensor_reduce(nmax2[par][:, 0:4], banks[0].rearrange("p (h s) -> p h s", s=128),
                                    axis=AX.X, op=ALU.max, negate=False)
                    sV.inc(v.tensor_reduce(nmax2[par][:, 4:8], banks[1].rearrange("p (h s) -> p h s", s=128),
                                           axis=AX.X, op=ALU.max, negate=False))
                V(_)
                st_nm[i] = sV.n

            def NTCEBM(i):
                par = i % 2
                banks = SCB[par]
                w_oe = st_oe[i - 2] if i >= 2 else 0

                def _(t, par=par, w=st_nm[i], w_oe=w_oe):
                    t.wait_ge(sV.h, w)
                    if w_oe:
                        t.wait_ge(sA.h, w_oe)
                    sP.inc(t.transpose(POB[par][0:8, 384:512], nmax2[par][:], identf[:]))
                T(_)
                p_nt = sP.n

                def _(s, par=par, p_nt=p_nt):
                    s.wait_ge(sP.h, p_nt)
                    sA.inc(s.copy(cTB2[par][:], POB[par][0:8, 384:512]))
                A(_)
                st_ct[i] = sA.n

                def _(t, par=par, banks=banks, a_ct=st_ct[i]):
                    t.wait_ge(sA.h, a_ct)
                    for j in range(2):
                        ii = t.matmul(banks[j][:, 0:512], cTB2[par][:],
                                      negB[:, j * 512:j * 512 + 512],
                                      start=False, stop=True, skip_group_check=True)
                    sP.inc(ii)
                T(_)
                st_bm[i] = sP.n

            def EXZS(i):
                par = i % 2
                banks = SCB[par]

                def _(s, par=par, banks=banks, w=st_bm[i]):
                    s.wait_ge(sP.h, w)
                    s.activation(attn2[par][:, 0:512], banks[0], AF.Exp, scale=0.125)
                    sA.inc(s.activation(attn2[par][:, 512:1024], banks[1], AF.Exp, scale=0.125))
                A(_)
                st_ex[i] = sA.n

                def _(v, par=par, w=st_ex[i]):
                    v.wait_ge(sA.h, w)
                    sV.inc(v.tensor_reduce(ZB2[par][:], attn2[par][:].rearrange("p (h s) -> p h s", s=128),
                                           axis=AX.X, op=ALU.add))
                    v.wait_ge(sV.h, sV.n)
                    sV.inc(v.reciprocal(RB2[par][:], ZB2[par][:]))
                    v.wait_ge(sV.h, sV.n)
                    sV.inc(v.tensor_copy(RB2b[par][:], RB2[par][:]))
                    v.wait_ge(sV.h, sV.n)
                    sV.inc(v.tensor_tensor(attn2[par][:].rearrange("p (h s) -> p h s", s=128),
                                           attn2[par][:].rearrange("p (h s) -> p h s", s=128),
                                           RB2b[par][:].unsqueeze(2).broadcast_to([P, 8, 128]), op=ALU.mult))
                V(_)
                st_at[i] = sV.n

            def ATOM(i):
                tt, grp = divmod(i, 2)
                par = i % 2
                w_ae = st_ae[i - 2] if i >= 2 else 0

                def _(t, par=par, w=st_at[i], w_ae=w_ae):
                    t.wait_ge(sV.h, w)
                    if w_ae:
                        t.wait_ge(sA.h, w_ae)
                    for hl in range(8):
                        ii = t.transpose(PTB[par][:, hl * 128:hl * 128 + 128],
                                         attn2[par][:, hl * 128:hl * 128 + 128], identb[:])
                    sP.inc(ii)
                T(_)
                p_at = sP.n

                def _(s, par=par, p_at=p_at):
                    s.wait_ge(sP.h, p_at)
                    sA.inc(s.copy(expT2[par][:], PTB[par][:]))
                A(_)
                st_ae[i] = sA.n

                def _(t, b=b, grp=grp, par=par, a_ae=st_ae[i], w_ct=st_ct[i]):
                    t.wait_ge(sA.h, a_ae)
                    for hl in range(8):
                        h = grp * 8 + hl
                        base = 64 * (h % 2)
                        cs = (h // 2) % 4
                        ec = (hl // 2) * 128 + (hl % 2) * 512
                        ii = t.matmul(POB[par][base:base + 64, cs * 128:cs * 128 + 128],
                                      bVb[b][:, h * 64:h * 64 + 64],
                                      expT2[par][:, ec:ec + 128],
                                      start=(hl in (0, 1)), stop=(hl in (6, 7)), skip_group_check=True)
                    sP.inc(ii)
                T(_)
                p_om = sP.n

                def _(s, tt=tt, grp=grp, par=par, p_om=p_om):
                    s.wait_ge(sP.h, p_om)
                    sA.inc(s.copy(aout_all[:, tt, grp * 512:grp * 512 + 512], POB[par][:, 0:512]))
                A(_)
                st_oe[i] = sA.n

            for i in range(nsl + 3):
                if 0 <= i - 2 < nsl:
                    EXZS(i - 2)
                if 0 <= i - 3 < nsl:
                    ATOM(i - 3)
                if 0 <= i - 1 < nsl:
                    NM(i - 1)
                    NTCEBM(i - 1)
                if i < nsl:
                    SC(i)
            a_aout_all = sA.n

            wob = [(pE[:, 0:512], pE[:, 512:1024]), (pA[:, 0:512], pB[:, 0:512])]
            for tt in range(NT):
                par = tt % 2
                banks = wob[par]
                w_prev = trk.get(f"b3free_{par}", 0)

                def _(t, tt=tt, banks=banks, w_prev=w_prev, a_aout_all=a_aout_all):
                    t.wait_ge(sA.h, a_aout_all)
                    if w_prev:
                        t.wait_ge(sV.h, w_prev)
                    for dh in range(2):
                        for kc in range(8):
                            t.matmul(banks[dh], aout_all[:, tt, kc * 128:kc * 128 + 128],
                                     wo_s[:, kc * 1024 + dh * 512:kc * 1024 + dh * 512 + 512],
                                     start=(kc == 0), stop=False, skip_group_check=True)
                        ii = t.matmul(banks[dh], onesb[:], bo_s[0:1, dh * 512:dh * 512 + 512],
                                      start=False, stop=True, skip_group_check=True)
                    sP.inc(ii)
                T(_)
                p_wo = sP.n

                def _(v, par=par, banks=banks, p_wo=p_wo):
                    v.wait_ge(sP.h, p_wo)
                    sV.inc(v.bn_stats(bnb2[par][:, 0:6], banks[0]))
                    sV.inc(v.bn_stats(bnb2[par][:, 6:12], banks[1]))
                    v.wait_ge(sV.h, sV.n)
                    sV.inc(v.bn_aggr(mv2[par][:], bnb2[par][:]))
                    v.wait_ge(sV.h, sV.n)
                    sV.inc(v.tensor_scalar(sc2[par][:, 0:1], mv2[par][:, 1:2], EPS, None, op0=ALU.add))
                V(_)
                v_u2 = sV.n

                def _(g, par=par, v_u2=v_u2):
                    g.wait_ge(sV.h, v_u2)
                    sG.inc(g.tensor_tensor(sc2[par][:, 3:4], sc2[par][:, 0:1], nh1_s[:], op=ALU.pow))
                G(_)
                g_r2 = sG.n

                xr_free = max(trk.get(f"xr_free_{par}", 0), trk.get(f"xinfree_{par}", 0))

                def _(q, b=b, tt=tt, par=par, xr_free=xr_free):
                    if xr_free:
                        q.wait_ge(sV.h, xr_free)
                    dXp[par].inc(q.dma_start(out=xin[par][:], in_=x_tile_v[b, tt]), 16)
                Q(_)
                xr_ld = dXp[par].n

                x2_free = trk.get(f"x2free_{par}", 0)

                def _(v, b=b, par=par, banks=banks, g_r2=g_r2, x2_free=x2_free, xr_ld=xr_ld):
                    v.wait_ge(dXp[par].h, xr_ld)
                    v.wait_ge(sG.h, g_r2)
                    if x2_free:
                        v.wait_ge(dSp[par].h, x2_free)
                    sV.inc(v.tensor_scalar(sc2[par][:, 1:2], mv2[par][:, 0:1], sc2[par][:, 3:4], -1.0,
                                           op0=ALU.mult, op1=ALU.mult))
                    v.wait_ge(sV.h, sV.n)
                    sV.inc(v.tensor_scalar(x2t[par][:, 0:512], banks[0], sc2[par][:, 3:4], sc2[par][:, 1:2],
                                           op0=ALU.mult, op1=ALU.add))
                    sV.inc(v.tensor_scalar(x2t[par][:, 512:1024], banks[1], sc2[par][:, 3:4], sc2[par][:, 1:2],
                                           op0=ALU.mult, op1=ALU.add))
                    v.wait_ge(sV.h, sV.n)
                    sV.inc(v.tensor_tensor(x2t[par][:], x2t[par][:], xin[par][:], op=ALU.add))
                V(_)
                v_x2 = sV.n
                trk[f"b3free_{par}"] = v_x2
                trk[f"xr_free_{par}"] = v_x2
                trk[f"xinfree_{par}"] = v_x2

                def _(q, b=b, tt=tt, par=par, v_x2=v_x2):
                    q.wait_ge(sV.h, v_x2)
                    dSp[par].inc(q.dma_start(out=x2_tile_v[b, tt], in_=x2t[par][:]), 16)
                Q(_)
                trk[f"x2free_{par}"] = dSp[par].n


        def barrier():
            vals = [(sP, sP.n), (sV, sV.n), (sA, sA.n), (sG, sG.n),
                    (dXA, dXA.n), (dXA2, dXA2.n), (dXp[0], dXp[0].n), (dXp[1], dXp[1].n),
                    (dCon, dCon.n), (dPro, dPro.n), (dWkv, dWkv.n), (dWqo, dWqo.n),
                    (dSp[0], dSp[0].n), (dSp[1], dSp[1].n)]

            def _(e):
                for s, vv in vals:
                    if vv:
                        e.wait_ge(s.h, vv)
            for body in (T, V, A, G, Q):
                body(_)

        if "A" in phases:
            emit_A_all()
        barrier()
        esA.close()

        esB = ExitStack()
        wq_s = esB.enter_context(nc.sbuf_tensor("wq_s", [128, 8192], BF16))
        wo_s = esB.enter_context(nc.sbuf_tensor("wo_s", [128, 8192], BF16))

        def _(g):
            g.dma_start(out=wq_s[:], in_=wq_d).then_inc(dWqo.h, 16)
            g.dma_start(out=wo_s[:], in_=wo_d).then_inc(dWqo.h, 16)
            dWqo.n += 32
        G(_)
        xn1f_all = esB.enter_context(nc.sbuf_tensor("xn1f_all", [P, NT, D], BF16))
        Q_all = esB.enter_context(nc.sbuf_tensor("Q_all", [P, NT, D], BF16))
        aout_all = esB.enter_context(nc.sbuf_tensor("aout_all", [P, NT, D], BF16))
        xnB = [esB.enter_context(nc.sbuf_tensor(f"xnB{i}", [P, D], BF16)) for i in range(2)]
        nmax2 = [esB.enter_context(nc.sbuf_tensor(f"nmax2_{i}", [P, 8], F32)) for i in range(2)]
        cTB2 = [esB.enter_context(nc.sbuf_tensor(f"cTB2_{i}", [8, 128], BF16)) for i in range(2)]
        ZB2 = [esB.enter_context(nc.sbuf_tensor(f"ZB2_{i}", [P, 8], F32)) for i in range(2)]
        RB2 = [esB.enter_context(nc.sbuf_tensor(f"RB2_{i}", [P, 8], F32)) for i in range(2)]
        RB2b = [esB.enter_context(nc.sbuf_tensor(f"RB2b_{i}", [P, 8], BF16)) for i in range(2)]
        attn2 = [esB.enter_context(nc.sbuf_tensor(f"attn2_{i}", [P, 1024], BF16)) for i in range(2)]
        expT2 = [esB.enter_context(nc.sbuf_tensor(f"expT2_{i}", [P, 1024], BF16)) for i in range(2)]
        bnb2 = [esB.enter_context(nc.sbuf_tensor(f"bnb2_{i}", [P, 12], F32)) for i in range(2)]
        mv2 = [esB.enter_context(nc.sbuf_tensor(f"mv2_{i}", [P, 2], F32)) for i in range(2)]
        sc2 = [esB.enter_context(nc.sbuf_tensor(f"sc2_{i}", [P, 4], F32)) for i in range(2)]
        bKTb = [esB.enter_context(nc.sbuf_tensor(f"bKTb{b}", [P, 2048], BF16)) for b in range(2)]
        bVb = [esB.enter_context(nc.sbuf_tensor(f"bVb{b}", [P, 1024], BF16)) for b in range(2)]
        xin = [esB.enter_context(nc.sbuf_tensor(f"xin{i}", [P, D], F32)) for i in range(2)]
        x2t = [esB.enter_context(nc.sbuf_tensor(f"x2t{i}", [P, D], F32)) for i in range(2)]

        if "B" in phases:
            emit_B(0)
            emit_B(1)
        barrier()
        esB.close()
        esAB.close()

        if "F" in phases:
            dFc = Sm(es, nc, "dFc")
            esF = ExitStack()
            muu = esF.enter_context(nc.sbuf_tensor("muu", [P, 2048], BF16))
            gam_s = esF.enter_context(nc.sbuf_tensor("gam_s", [P, D], F32))
            bet_s = esF.enter_context(nc.sbuf_tensor("bet_s", [P, D], F32))
            scl_s = esF.enter_context(nc.sbuf_tensor("scl_s", [P, 1], F32))
            b1f_s = esF.enter_context(nc.sbuf_tensor("b1f_s", [P, 32], F32))
            b2r_s = esF.enter_context(nc.sbuf_tensor("b2r_s", [P, 2, D], BF16))
            esFpre = ExitStack()
            mu_sb = esFpre.enter_context(nc.sbuf_tensor("mu_sb", [P, 2048], F32))
            musq = esFpre.enter_context(nc.sbuf_tensor("musq", [P, 2048], BF16))
            ssmu = esFpre.enter_context(nc.sbuf_tensor("ssmu", [1, 1024], F32))
            rmub = esFpre.enter_context(nc.sbuf_tensor("rmub", [P, 256], F32))
            def _(g):
                g.dma_start(out=mu_sb[:], in_=mu_d).then_inc(dFc.h, 16)
                g.dma_start(out=gam_s[:], in_=gam_d).then_inc(dFc.h, 16)
                g.dma_start(out=bet_s[:], in_=bet_d).then_inc(dFc.h, 16)
                g.dma_start(out=scl_s[:], in_=scl_d).then_inc(dFc.h, 16)
                g.dma_start(out=b1f_s[:], in_=b1_d).then_inc(dFc.h, 16)
                g.dma_start(out=b2r_s[:], in_=b2_d.rearrange("(c p) d -> p c d", p=P)).then_inc(dFc.h, 16)
                dFc.n += 96
            G(_)
            w_fc = dFc.n

            def _(s):
                s.wait_ge(dFc.h, w_fc)
                sA.inc(s.activation(musq[:], mu_sb[:], AF.Square))
            A(_)
            a_musq = sA.n

            def _(t):
                t.wait_ge(sA.h, a_musq)
                for kc in range(8):
                    ii = t.matmul(pD[0:1, 0:256], onescolb[:], musq[:, kc * 256:kc * 256 + 256],
                                  start=(kc == 0), stop=(kc == 7))
                sP.inc(ii)
            T(_)
            p_mu = sP.n

            def _(s):
                s.wait_ge(sP.h, p_mu)
                sA.inc(s.copy(ssmu[0:1, 0:256], pD[0:1, 0:256]))
                s.wait_ge(sA.h, sA.n)
                sA.inc(s.activation(ssmu[0:1, 256:512], ssmu[0:1, 0:256], AF.Ln))
                s.wait_ge(sA.h, sA.n)
                sA.inc(s.activation(ssmu[0:1, 512:768], ssmu[0:1, 256:512], AF.Exp, scale=0.5))
            A(_)
            a_nmu = sA.n

            def _(v):
                v.wait_ge(sA.h, a_nmu)
                sV.inc(v.tensor_scalar(ssmu[0:1, 768:1024], ssmu[0:1, 512:768], 1e-12, None, op0=ALU.max))
                v.wait_ge(sV.h, sV.n)
                sV.inc(v.reciprocal(ssmu[0:1, 0:256], ssmu[0:1, 768:1024]))
            V(_)
            v_rmu = sV.n

            def _(t):
                t.wait_ge(sV.h, v_rmu)
                sP.inc(t.matmul(pD[0:128, 0:256], ones1f[:], ssmu[0:1, 0:256], start=True, stop=True))
            T(_)
            p_rmub = sP.n

            def _(s):
                s.wait_ge(sP.h, p_rmub)
                sA.inc(s.copy(rmub[:], pD[0:128, 0:256]))
            A(_)
            a_rmub = sA.n

            def _(v):
                v.wait_ge(sA.h, a_rmub)
                for kc in range(8):
                    sV.inc(v.tensor_tensor(muu[:, kc * 256:kc * 256 + 256], mu_sb[:, kc * 256:kc * 256 + 256],
                                           rmub[:], op=ALU.mult))
            V(_)
            v_muu = sV.n

            esFpre.close()
            xnn_sb = esF.enter_context(nc.sbuf_tensor("xnn_sb", [P, NT, D], BF16))
            lgsm = esF.enter_context(nc.sbuf_tensor("lgsm", [P, 2, 2048], BF16))
            comb = esF.enter_context(nc.sbuf_tensor("comb", [P, 2, NT, 256], BF16))
            si_sb = esF.enter_context(nc.sbuf_tensor("si_sb", [P, 2, 8, 256], BF16))
            so_stg = esF.enter_context(nc.sbuf_tensor("so_stg", [P, 2, 2, D], BF16))
            so_bf = esF.enter_context(nc.sbuf_tensor("so_bf", [P, 2, 2, D], BF16))
            w1buf = [esF.enter_context(nc.sbuf_tensor(f"w1b{i}", [P, 2048], BF16)) for i in range(4)]
            w2buf = [esF.enter_context(nc.sbuf_tensor(f"w2b{i}", [P, 2048], BF16)) for i in range(4)]
            so_st = [esF.enter_context(nc.sbuf_tensor(f"sost{i}", [32, 1024], F32)) for i in range(2)]
            h_bf = esF.enter_context(nc.sbuf_tensor("h_bf", [P, 64], BF16))
            x2f = [esF.enter_context(nc.sbuf_tensor(f"x2f{i}", [P, D], F32)) for i in range(2)]
            t1f = esF.enter_context(nc.sbuf_tensor("t1f", [P, D], F32))
            xu_tp = [esF.enter_context(nc.sbuf_tensor(f"xu_tp{i}", [P, D], BF16)) for i in range(2)]
            xufp = [esF.enter_context(nc.sbuf_tensor(f"xufp{i}", [P, D], BF16)) for i in range(2)]
            lg_tp = [esF.enter_context(nc.sbuf_tensor(f"lg_tp{i}", [P, 256], F32)) for i in range(2)]
            expcp = [esF.enter_context(nc.sbuf_tensor(f"expcp{i}", [P, 256], F32)) for i in range(2)]
            scT = esF.enter_context(nc.sbuf_tensor("scT", [P, NT, 16], F32))
            mvT = esF.enter_context(nc.sbuf_tensor("mvT", [P, NT, 4], F32))
            bnF = esF.enter_context(nc.sbuf_tensor("bnF", [P, 12], F32))
            mvF = esF.enter_context(nc.sbuf_tensor("mvF", [P, 2], F32))
            scF = esF.enter_context(nc.sbuf_tensor("scF", [P, 16], F32))
            scD = esF.enter_context(nc.sbuf_tensor("scD", [P, 4], F32))
            scC = esF.enter_context(nc.sbuf_tensor("scC", [P, 4], F32))
            dspb = esF.enter_context(nc.sbuf_tensor("dspb", [P, 2048], BF16))
            esFd = ExitStack()
            dtm = esFd.enter_context(nc.sbuf_tensor("dtm", [P, NT, 256], BF16))

            ftrk = {"ptrb": 0, "ptrf": 0, "pA": 0, "si": {}, "comb": {}}

            fst = ftrk.setdefault("fst", {})

            def fget(name, b, t):
                if t >= 0:
                    return fst.get((name, b, t), 0)
                return fst.get((name, b - 1, NT + t), 0)

            def fS1(b, t):
                par = t % 2
                xfree = ftrk.get(f"x2f{par}", 0)

                def _(q, b=b, t=t, par=par, xfree=xfree):
                    if xfree:
                        q.wait_ge(sV.h, xfree)
                    if b == 0 and t < 2:
                        q.wait_ge(sV.h, v_muu)
                    dXp[par].inc(q.dma_start(out=x2f[par][:], in_=x2_tile_v[b, t]), 16)
                Q(_)
                x_ld = dXp[par].n

                def _(v, t=t, par=par, x_ld=x_ld):
                    v.wait_ge(dXp[par].h, x_ld)
                    sV.inc(v.bn_stats(bnF[:, 0:6], x2f[par][:, 0:512]))
                    sV.inc(v.bn_stats(bnF[:, 6:12], x2f[par][:, 512:1024]))
                    v.wait_ge(sV.h, sV.n)
                    sV.inc(v.bn_aggr(mvT[:, t, 0:2], bnF[:]))
                    v.wait_ge(sV.h, sV.n)
                    sV.inc(v.tensor_scalar(scT[:, t, 0:1], mvT[:, t, 1:2], EPS, None, op0=ALU.add))
                V(_)
                v_u3 = sV.n

                def _(g, t=t, v_u3=v_u3):
                    g.wait_ge(sV.h, v_u3)
                    g.wait_ge(dFc.h, w_fc)
                    sG.inc(g.tensor_tensor(scT[:, t, 1:2], scT[:, t, 0:1], nh1_s[:], op=ALU.pow))
                G(_)
                fst[("r3", b, t)] = sG.n

            def fS3(b, t):
                w = fst[("r3", b, t)]

                def _(v, t=t, w=w):
                    v.wait_ge(sG.h, w)
                    sV.inc(v.tensor_scalar(scT[:, t, 2:3], mvT[:, t, 1:2], scT[:, t, 1:2], scT[:, t, 1:2],
                                           op0=ALU.mult, op1=ALU.mult))
                    v.wait_ge(sV.h, sV.n)
                    sV.inc(v.tensor_scalar(scT[:, t, 3:4], scT[:, t, 2:3], EPS, None, op0=ALU.add))
                V(_)
                vv = sV.n

                def _(g, t=t, vv=vv):
                    g.wait_ge(sV.h, vv)
                    sG.inc(g.tensor_tensor(scT[:, t, 4:5], scT[:, t, 3:4], nh1_s[:], op=ALU.pow))
                G(_)
                fst[("r4", b, t)] = sG.n

            def fS5(b, t):
                par = t % 2
                w_r4 = fst[("r4", b, t)]
                si_mm = ftrk.get("si_mm_done", 0) if t == 0 else 0

                def _(v, b=b, t=t, par=par, w_r4=w_r4, si_mm=si_mm):
                    v.wait_ge(sG.h, w_r4)
                    if si_mm:
                        v.wait_ge(sP.h, si_mm)
                    sV.inc(v.tensor_tensor(scT[:, t, 5:6], scT[:, t, 1:2], scT[:, t, 4:5], op=ALU.mult))
                    v.wait_ge(sV.h, sV.n)
                    sV.inc(v.tensor_scalar(scT[:, t, 6:7], mvT[:, t, 0:1], scT[:, t, 5:6], -1.0,
                                           op0=ALU.mult, op1=ALU.mult))
                    v.wait_ge(sV.h, sV.n)
                    sV.inc(v.tensor_scalar(t1f[:], x2f[par][:], scT[:, t, 5:6], scT[:, t, 6:7],
                                           op0=ALU.mult, op1=ALU.add))
                    v.wait_ge(sV.h, sV.n)
                    sV.inc(v.tensor_tensor(t1f[:], t1f[:], gam_s[:], op=ALU.mult))
                    v.wait_ge(sV.h, sV.n)
                    sV.inc(v.tensor_tensor(xnn_sb[:, t, :], t1f[:], bet_s[:], op=ALU.add))
                    v.wait_ge(sV.h, sV.n)
                    sV.inc(v.bn_stats(bnF[:, 0:6], xnn_sb[:, t, 0:512]))
                    sV.inc(v.bn_stats(bnF[:, 6:12], xnn_sb[:, t, 512:1024]))
                    v.wait_ge(sV.h, sV.n)
                    sV.inc(v.bn_aggr(mvT[:, t, 2:4], bnF[:]))
                    v.wait_ge(sV.h, sV.n)
                    sV.inc(v.tensor_scalar(scT[:, t, 7:8], mvT[:, t, 2:3], mvT[:, t, 2:3], mvT[:, t, 3:4],
                                           op0=ALU.mult, op1=ALU.add))
                    v.wait_ge(sV.h, sV.n)
                    sV.inc(v.tensor_scalar(scT[:, t, 8:9], scT[:, t, 7:8], 1024.0, None, op0=ALU.mult))
                V(_)
                v_ss = sV.n
                ftrk[f"x2f{par}"] = v_ss

                def _(g, t=t, v_ss=v_ss):
                    g.wait_ge(sV.h, v_ss)
                    sG.inc(g.tensor_tensor(scT[:, t, 9:10], scT[:, t, 8:9], ph1_s[:], op=ALU.pow))
                G(_)
                fst[("n5", b, t)] = sG.n

            def fS7(b, t):
                par = t % 2
                w_n5 = fst[("n5", b, t)]
                w_xuT_old = fget("xuT", b, t - 2)

                def _(v, t=t, par=par, w_n5=w_n5, w_xuT_old=w_xuT_old):
                    v.wait_ge(sG.h, w_n5)
                    if w_xuT_old:
                        v.wait_ge(sP.h, w_xuT_old)
                    sV.inc(v.tensor_scalar(scT[:, t, 10:11], scT[:, t, 9:10], 1e-12, None, op0=ALU.max))
                    v.wait_ge(sV.h, sV.n)
                    sV.inc(v.reciprocal(scT[:, t, 11:12], scT[:, t, 10:11]))
                    v.wait_ge(sV.h, sV.n)
                    sV.inc(v.tensor_tensor(scT[:, t, 12:13], scT[:, t, 11:12], scl_s[:], op=ALU.mult))
                    v.wait_ge(sV.h, sV.n)
                    sV.inc(v.tensor_scalar(xu_tp[par][:], xnn_sb[:, t, :], scT[:, t, 12:13], None, op0=ALU.mult))
                V(_)
                v_xu = sV.n
                w_pb_free = fget("xufe", b, t - 1)

                def _(t_, t=t, par=par, v_xu=v_xu, w_pb_free=w_pb_free):
                    t_.wait_ge(sV.h, v_xu)
                    if w_pb_free:
                        t_.wait_ge(sA.h, w_pb_free)
                    for kc in range(8):
                        ii = t_.transpose(ptrbB[:, kc * 128:kc * 128 + 128], xu_tp[par][:, kc * 128:kc * 128 + 128],
                                          identb[:])
                    sP.inc(ii)
                T(_)
                fst[("xuT", b, t)] = sP.n
                p_xuT = sP.n
                w_lgmm_old = fget("lgmm", b, t - 2)

                def _(s, par=par, p_xuT=p_xuT, w_lgmm_old=w_lgmm_old):
                    s.wait_ge(sP.h, p_xuT)
                    if w_lgmm_old:
                        s.wait_ge(sP.h, w_lgmm_old)
                    sA.inc(s.copy(xufp[par][:], ptrbB[:]))
                A(_)
                fst[("xufe", b, t)] = sA.n

            def fS9(b, t):
                par = t % 2
                bank = pA if par == 0 else pB
                w_xuf = fst[("xufe", b, t)]
                w_bank = fget("lge", b, t - 2) or ftrk.get(f"pA{par}", 0)
                w_si0 = ftrk.get("si_done0", 0) if (b == 1 and t == par) else 0

                def _(t_, t=t, par=par, bank=bank, w_xuf=w_xuf, w_bank=w_bank, w_si0=w_si0):
                    t_.wait_ge(sA.h, w_xuf)
                    t_.wait_ge(sV.h, v_muu)
                    if w_bank:
                        t_.wait_ge(sA.h, w_bank)
                    if w_si0:
                        t_.wait_ge(sA.h, w_si0)
                    for kc in range(8):
                        t_.matmul(bank[:, 0:256], xufp[par][:, kc * 128:kc * 128 + 128],
                                  muu[:, kc * 256:kc * 256 + 256], start=(kc == 0), stop=(kc == 7),
                                  skip_group_check=True)
                    for sh in range(2):
                        for kc in range(8):
                            ii = t_.matmul(bank[:, 256 + sh * 128:384 + sh * 128],
                                           muu[:, kc * 256 + sh * 128:kc * 256 + sh * 128 + 128],
                                           xufp[par][:, kc * 128:kc * 128 + 128],
                                           start=(kc == 0), stop=(kc == 7), skip_group_check=True)
                    sP.inc(ii)
                T(_)
                fst[("lgmm", b, t)] = sP.n
                p_lg = sP.n
                w_nm_old = fget("nmc", b, t - 2)
                disp = ftrk.get("disp_done_v", 0)

                def _(s, t=t, par=par, bank=bank, p_lg=p_lg, w_nm_old=w_nm_old, disp=disp):
                    s.wait_ge(sP.h, p_lg)
                    if w_nm_old:
                        s.wait_ge(sV.h, w_nm_old)
                    sA.inc(s.copy(lg_tp[par][:], bank[:, 0:256]))
                    if disp:
                        s.wait_ge(sV.h, disp)
                    sA.inc(s.copy(lgsm[:, :, t * 128:t * 128 + 128],
                                  bank[:, 256:512].rearrange("p (c t) -> p c t", t=128)))
                A(_)
                fst[("lge", b, t)] = sA.n
                ftrk[f"pA{par}"] = sA.n

            def fS10a(b, t):
                par = t % 2
                w_lge = fst[("lge", b, t)]

                def _(v, t=t, par=par, w_lge=w_lge):
                    v.wait_ge(sA.h, w_lge)
                    sV.inc(v.tensor_reduce(scT[:, t, 13:14], lg_tp[par][:], axis=AX.X, op=ALU.max, negate=True))
                V(_)
                fst[("nmc", b, t)] = sV.n
                v_nmc = sV.n

                def _(s, t=t, par=par, v_nmc=v_nmc):
                    s.wait_ge(sV.h, v_nmc)
                    sA.inc(s.activation(expcp[par][:], lg_tp[par][:], AF.Exp, bias=scT[:, t, 13:14],
                                        accum_out=scT[:, t, 14:15]))
                A(_)
                fst[("expc", b, t)] = sA.n

            def fS10b(b, t):
                par = t % 2
                a_expc = fst[("expc", b, t)]

                def _(v, b=b, t=t, par=par, a_expc=a_expc):
                    v.wait_ge(sA.h, a_expc)
                    sV.inc(v.reciprocal(scT[:, t, 15:16], scT[:, t, 14:15]))
                    v.wait_ge(sV.h, sV.n)
                    sV.inc(v.tensor_scalar(comb[:, b, t, :], expcp[par][:], scT[:, t, 15:16], None, op0=ALU.mult))
                V(_)
                ftrk["comb"][b] = sV.n

            def emit_F1(b):
                for i in range(NT + 6):
                    if 0 <= i - 2 < NT:
                        fS5(b, i - 2)
                    if 0 <= i - 3 < NT:
                        fS7(b, i - 3)
                    if 0 <= i - 4 < NT:
                        fS9(b, i - 4)
                    if 0 <= i - 5 < NT:
                        fS10a(b, i - 5)
                    if 0 <= i - 6 < NT:
                        fS10b(b, i - 6)
                    if i < NT:
                        fS1(b, i)
                    if 0 <= i - 1 < NT:
                        fS3(b, i - 1)
                ftrk["lgsm_done"] = sA.n
                ftrk["xnn_done"] = sV.n
                if tuple() and dbg_names:
                    pass

                for c in range(2):
                    prev_dsp = ftrk.get("a_dsp_last", 0)

                    def _(v, c=c, prev_dsp=prev_dsp):
                        v.wait_ge(sA.h, ftrk["lgsm_done"])
                        if prev_dsp:
                            v.wait_ge(sA.h, prev_dsp)
                        sV.inc(v.tensor_reduce(scD[:, 0:1], lgsm[:, c, :], axis=AX.X, op=ALU.max, negate=True))
                    V(_)
                    v_nmd = sV.n

                    dspb_free = ftrk.get("dspb_free", 0)

                    def _(s, c=c, v_nmd=v_nmd, dspb_free=dspb_free):
                        s.wait_ge(sV.h, v_nmd)
                        if dspb_free:
                            s.wait_ge(sP.h, dspb_free)
                        sA.inc(s.activation(dspb[:], lgsm[:, c, :], AF.Exp, bias=scD[:, 0:1], accum_out=scD[:, 1:2]))
                        s.wait_ge(sA.h, sA.n)
                        sA.inc(s.activation(scD[:, 2:3], scD[:, 1:2], AF.Ln))
                    A(_)
                    a_lnz = sA.n

                    def _(v, a_lnz=a_lnz):
                        v.wait_ge(sA.h, a_lnz)
                        sV.inc(v.tensor_scalar(scD[:, 3:4], scD[:, 2:3], -1.0, scD[:, 0:1], op0=ALU.mult, op1=ALU.add))
                    V(_)
                    v_b2x = sV.n
                    ftrk["v_b2x"] = v_b2x

                    def _(s, c=c, v_b2x=v_b2x):
                        s.wait_ge(sV.h, v_b2x)
                        sA.inc(s.activation(dspb[:], lgsm[:, c, :], AF.Exp, bias=scD[:, 3:4]))
                    A(_)
                    v_dsp = sV.n
                    a_dsp = sA.n
                    ftrk["a_dsp_last"] = a_dsp

                    for rnd in range(2):
                        ptrb_free = ftrk["ptrb"]

                        def _(t, c=c, rnd=rnd, a_dsp=a_dsp, ptrb_free=ptrb_free):
                            t.wait_ge(sA.h, a_dsp)
                            if ptrb_free:
                                t.wait_ge(sA.h, ptrb_free)
                            for j in range(8):
                                tt = rnd * 8 + j
                                ii = t.transpose(ptrb[:, j * 128:j * 128 + 128],
                                                 dspb[:, tt * 128:tt * 128 + 128], identb[:])
                            sP.inc(ii)
                        T(_)
                        p_dT = sP.n
                        ftrk["dspb_free"] = p_dT
                        si_mm_done = ftrk.get("si_mm_done", 0)

                        def _(s, c=c, rnd=rnd, p_dT=p_dT, si_mm_done=si_mm_done):
                            s.wait_ge(sP.h, p_dT)
                            if si_mm_done:
                                s.wait_ge(sP.h, si_mm_done)
                            sA.inc(s.copy(dtm[:, rnd * 8:rnd * 8 + 8, c * 128:c * 128 + 128],
                                          ptrb[:].rearrange("p (j t) -> p j t", t=128)))
                        A(_)
                        ftrk["ptrb"] = sA.n
                v_dtm = sA.n
                ftrk["disp_done_v"] = sV.n
                if dbg_names:
                    dump(f"dtmA_b{b}", dtm[:, 0, :], sA, v_dtm)

                prev_si = 0
                for dc in range(8):
                    bank = pB if dc % 2 == 0 else pC

                    def _(t, b=b, dc=dc, bank=bank, v_dtm=v_dtm, prev_si=prev_si):
                        t.wait_ge(sA.h, v_dtm)
                        t.wait_ge(sV.h, ftrk["xnn_done"])
                        if prev_si:
                            t.wait_ge(sA.h, prev_si)
                        for tt in range(NT):
                            ii = t.matmul(bank[:, 0:256], xnn_sb[:, tt, dc * 128:dc * 128 + 128],
                                          dtm[:, tt, :], start=(tt == 0), stop=(tt == NT - 1))
                        sP.inc(ii)
                    T(_)
                    p_si = sP.n

                    def _(s, b=b, dc=dc, bank=bank, p_si=p_si):
                        s.wait_ge(sP.h, p_si)
                        sA.inc(s.copy(si_sb[:, b, dc, :], bank[:, 0:256]))
                    A(_)
                    prev_si = sA.n if dc >= 1 else 0
                    if dc >= 1:
                        prev_si = ftrk["si"].get(dc - 1, 0)
                    ftrk["si"][dc] = sA.n
                ftrk[f"si_done{b}"] = sA.n
                ftrk["si_mm_done"] = sP.n
                if dbg_names:
                    dump(f"dtmB_b{b}", dtm[:, 0, :], sP, sP.n)
                if dbg_names:
                    dump(f"xnn14_b_b{b}", xnn_sb[:, 14, :], sP, sP.n)
                if dbg_names:
                    dump(f"si_b{b}", si_sb[:, b], sA, sA.n)
                    dump(f"dtm_b{b}", dtm[:, 0, :], sA, sA.n)
                    dump(f"dtm7_b{b}", dtm[:, 7, :], sA, sA.n)
                    dump(f"dtm8_b{b}", dtm[:, 8, :], sA, sA.n)
                    dump(f"dtm15_b{b}", dtm[:, 15, :], sA, sA.n)

            def emit_wload(e):
                par = e % 4
                wfree = ftrk.get(f"wbuf{par}", 0)

                def _(g, e=e, par=par, wfree=wfree):
                    if wfree:
                        g.wait_ge(sP.h, wfree)
                    if e < 4:
                        g.wait_ge(sV.h, v_muu)
                    g.dma_start(out=w1buf[par][:], in_=w1_d[e]).then_inc(dE[par].h, 16)
                    g.dma_start(out=w2buf[par][:], in_=w2_d[e]).then_inc(dE[par].h, 16)
                    dE[par].n += 32
                G(_)
                ftrk[f"wload{e}"] = dE[par].n

            def emit_experts():
                si_done = max(ftrk["si_done0"], ftrk["si_done1"])
                for e in range(E):
                    par = e % 4
                    w_e = ftrk[f"wload{e}"]

                    pd_free = ftrk.get("pD_exp", 0)

                    def _(t, e=e, par=par, w_e=w_e, si_done=si_done, pd_free=pd_free):
                        t.wait_ge(dE[par].h, w_e)
                        t.wait_ge(sA.h, si_done)
                        if pd_free:
                            t.wait_ge(sA.h, pd_free)
                        for dc in range(8):
                            for hc in range(2):
                                ii = t.matmul(pD[:, hc * 32:hc * 32 + 32],
                                              w1buf[par][:, dc * 256 + hc * 128:dc * 256 + hc * 128 + 128],
                                              si_sb[:, :, dc, e * 16:e * 16 + 16],
                                              start=(dc == 0 and hc == 0), stop=(dc == 7 and hc == 1),
                                              skip_group_check=True)
                        sP.inc(ii)
                    T(_)
                    p_h = sP.n

                    def _(s, e=e, p_h=p_h):
                        s.wait_ge(sP.h, p_h)
                        sA.inc(s.activation(h_bf[:, 0:32], pD[:, 0:32], AF.Gelu, bias=b1f_s[:, 2 * e:2 * e + 1]))
                        sA.inc(s.activation(h_bf[:, 32:64], pD[:, 32:64], AF.Gelu, bias=b1f_s[:, 2 * e + 1:2 * e + 2]))
                    A(_)
                    a_h = sA.n
                    ftrk["pD_exp"] = a_h

                    pe_free = ftrk.get("pE_exp", 0)

                    def _(t, e=e, par=par, a_h=a_h, pe_free=pe_free):
                        t.wait_ge(sA.h, a_h)
                        if pe_free:
                            t.wait_ge(sA.h, pe_free)
                        for dh in range(2):
                            for hc in range(2):
                                ii = t.matmul(pE[0:32, dh * 512:dh * 512 + 512],
                                              h_bf[:, hc * 32:hc * 32 + 32],
                                              w2buf[par][:, hc * 1024 + dh * 512:hc * 1024 + dh * 512 + 512],
                                              start=(hc == 0), stop=(hc == 1), skip_group_check=True)
                        sP.inc(ii)
                    T(_)
                    p_so = sP.n
                    ftrk[f"wbuf{par}"] = p_so
                    if e + 4 < E:
                        emit_wload(e + 4)

                    spar = e % 2
                    stfree = ftrk.get(f"sost{spar}", 0)

                    def _(s, e=e, spar=spar, p_so=p_so, stfree=stfree):
                        s.wait_ge(sP.h, p_so)
                        if stfree:
                            s.wait_ge(dSh[spar].h, stfree)
                        sA.inc(s.copy(so_st[spar][0:32, 0:512], pE[0:32, 0:512]))
                        sA.inc(s.copy(so_st[spar][0:32, 512:1024], pE[0:32, 512:1024]))
                    A(_)
                    a_so = sA.n
                    ftrk["pE_exp"] = a_so

                    def _(g, e=e, spar=spar, a_so=a_so):
                        g.wait_ge(sA.h, a_so)
                        r0 = (e % 8) * 16
                        g.dma_start(out=so_stg[r0:r0 + 16, 0, e // 8, :], in_=so_st[spar][0:16, :]).then_inc(dSh[spar].h, 16)
                        g.dma_start(out=so_stg[r0:r0 + 16, 1, e // 8, :], in_=so_st[spar][16:32, :]).then_inc(dSh[spar].h, 16)
                        dSh[spar].n += 32
                    G(_)
                    ftrk[f"sost{spar}"] = dSh[spar].n

                def _(v):
                    v.wait_ge(dSh[0].h, dSh[0].n)
                    v.wait_ge(dSh[1].h, dSh[1].n)
                    for bb in range(2):
                        for c in range(2):
                            sV.inc(v.tensor_tensor(so_bf[:, bb, c, :], so_stg[:, bb, c, :],
                                                   b2r_s[:, c, :], op=ALU.add))
                V(_)
                ftrk["so_ready"] = sV.n
                if dbg_names:
                    dump("sobf_b0", so_bf[:, 0], sV, sV.n)
                    dump("sobf_b1", so_bf[:, 1], sV, sV.n)

            def emit_F3(b):
                for tt in range(NT):
                    par = tt % 2
                    xfree = max(ftrk.get(f"x2r{par}", 0), ftrk.get(f"x2f{par}", 0))

                    def _(q, b=b, tt=tt, par=par, xfree=xfree):
                        if xfree:
                            q.wait_ge(sV.h, xfree)
                        dXp[par].inc(q.dma_start(out=x2f[par][:], in_=x2_tile_v[b, tt]), 16)
                    Q(_)
                    x_ld = dXp[par].n

                    ptrb_free = ftrk["ptrb"]

                    def _(t, b=b, tt=tt, ptrb_free=ptrb_free):
                        t.wait_ge(sV.h, ftrk["comb"][b])
                        if ptrb_free:
                            t.wait_ge(sA.h, ptrb_free)
                        t.transpose(ptrb[:, 0:128], comb[:, b, tt, 0:128], identb[:])
                        sP.inc(t.transpose(ptrb[:, 128:256], comb[:, b, tt, 128:256], identb[:]))
                    T(_)
                    p_cT = sP.n

                    def _(s, p_cT=p_cT):
                        s.wait_ge(sP.h, p_cT)
                        sA.inc(s.copy(cb_sm[:], ptrb[:, 0:256]))
                    A(_)
                    a_cb = sA.n
                    ftrk["ptrb"] = a_cb

                    pbc_free = ftrk.get("pBC_F3", 0)

                    def _(t, b=b, a_cb=a_cb, pbc_free=pbc_free):
                        t.wait_ge(sA.h, a_cb)
                        t.wait_ge(sV.h, ftrk["so_ready"])
                        if pbc_free:
                            t.wait_ge(sV.h, pbc_free)
                        for dh in range(2):
                            bank = pB if dh == 0 else pC
                            for c in range(2):
                                ii = t.matmul(bank[:, 0:512], cb_sm[:, c * 128:c * 128 + 128],
                                              so_bf[:, b, c, dh * 512:dh * 512 + 512],
                                              start=(c == 0), stop=(c == 1), skip_group_check=True)
                        sP.inc(ii)
                    T(_)
                    p_o = sP.n

                    def _(v, p_o=p_o):
                        v.wait_ge(sP.h, p_o)
                        sV.inc(v.bn_stats(bnF[:, 0:6], pB[:, 0:512]))
                        sV.inc(v.bn_stats(bnF[:, 6:12], pC[:, 0:512]))
                        v.wait_ge(sV.h, sV.n)
                        sV.inc(v.bn_aggr(mvF[:], bnF[:]))
                        v.wait_ge(sV.h, sV.n)
                        sV.inc(v.tensor_scalar(scF[:, 0:1], mvF[:, 1:2], EPS, None, op0=ALU.add))
                    V(_)
                    v_u6 = sV.n

                    def _(g, v_u6=v_u6):
                        g.wait_ge(sV.h, v_u6)
                        sG.inc(g.tensor_tensor(scF[:, 3:4], scF[:, 0:1], nh1_s[:], op=ALU.pow))
                    G(_)
                    a_r6 = None
                    g_r6 = sG.n

                    ofree = ftrk.get(f"ostore{par}", 0)

                    def _(v, b=b, tt=tt, par=par, g_r6=g_r6, x_ld=x_ld, ofree=ofree):
                        v.wait_ge(sG.h, g_r6)
                        v.wait_ge(dXp[par].h, x_ld)
                        if ofree:
                            v.wait_ge(dOp[par].h, ofree)
                        sV.inc(v.tensor_scalar(scF[:, 4:5], mvF[:, 0:1], scF[:, 3:4], -1.0, op0=ALU.mult, op1=ALU.mult))
                        v.wait_ge(sV.h, sV.n)
                        sV.inc(v.tensor_scalar(ln5_t[:, 0:512], pB[:, 0:512], scF[:, 3:4], scF[:, 4:5], op0=ALU.mult, op1=ALU.add))
                        sV.inc(v.tensor_scalar(ln5_t[:, 512:1024], pC[:, 0:512], scF[:, 3:4], scF[:, 4:5], op0=ALU.mult, op1=ALU.add))
                        v.wait_ge(sV.h, sV.n)
                        sV.inc(v.tensor_tensor(out_t[par][:], x2f[par][:], ln5_t[:], op=ALU.add))
                    V(_)
                    v_out = sV.n
                    ftrk["pBC_F3"] = v_out
                    ftrk[f"x2r{par}"] = v_out

                    def _(q, b=b, tt=tt, par=par, v_out=v_out):
                        q.wait_ge(sV.h, v_out)
                        dOp[par].inc(q.dma_start(out=out_tile_v[b, tt], in_=out_t[par][:]), 16)
                    Q(_)
                    ftrk[f"ostore{par}"] = dOp[par].n

            for _e in range(4):
                emit_wload(_e)
            emit_F1(0)
            emit_F1(1)
            esFd.close()
            cb_sm = esF.enter_context(nc.sbuf_tensor("cb_sm", [P, 256], BF16))
            ln5_t = esF.enter_context(nc.sbuf_tensor("ln5_t", [P, D], F32))
            out_t = [esF.enter_context(nc.sbuf_tensor(f"outt{i}", [P, D], F32)) for i in range(2)]
            emit_experts()
            emit_F3(0)
            emit_F3(1)
            esF.close()

        x2_done = (dSp[0].n, dSp[1].n)

        def _(q):
            if x2_done[0]:
                q.wait_ge(dSp[0].h, x2_done[0])
            if x2_done[1]:
                q.wait_ge(dSp[1].h, x2_done[1])
            if "B" in phases and "F" not in phases:
                for b in range(2):
                    dO.inc(q.dma_start(out=out_d[b], in_=x2scr[b]), 16)
            if dO.n:
                q.wait_ge(dO.h, dO.n)
            if dOp[0].n:
                q.wait_ge(dOp[0].h, dOp[0].n)
            if dOp[1].n:
                q.wait_ge(dOp[1].h, dOp[1].n)
            if dDbg.n:
                q.wait_ge(dDbg.h, dDbg.n)
        Q(_)

    f = nc.m.functions[0]
    blocks = f.blocks
    if len(blocks) > 2:
        def ekey(bb):
            engs = {str(getattr(ins, "engine", None)) for ins in bb.instructions}
            return sorted(engs)[0] if engs else "zz"
        f.blocks = [blocks[0]] + sorted(blocks[1:-1], key=ekey) + [blocks[-1]]
    return nc, dbg_outs


def _host_prep(inp):
    f = np.float32
    Wq, Wk, Wv, Wo = (np.asarray(inp[k], f) for k in ("L0_attn_Wq", "L0_attn_Wk", "L0_attn_Wv", "L0_attn_Wo"))

    def wlay(W):
        return np.ascontiguousarray(
            W.T.reshape(8, 128, 8, 128).transpose(1, 0, 2, 3).reshape(128, 8192))

    planes_T = np.asarray(inp["L0_attn_planes_T"], f)
    negA = np.zeros((64, 1024), f)
    for j in range(64):
        negA[j, j * 16:(j + 1) * 16] = -1.0
    negB = np.zeros((8, 1024), f)
    for j in range(8):
        negB[j, j * 128:(j + 1) * 128] = -1.0
    mu = np.asarray(inp["L0_ffn_mu"], f).reshape(1024, 256)
    w1 = np.asarray(inp["L0_ffn_w1"], f)
    w2 = np.asarray(inp["L0_ffn_w2"], f)
    b1 = np.asarray(inp["L0_ffn_b1"], f)
    com = {
        "wq": wlay(Wq), "wk": wlay(Wk), "wv": wlay(Wv), "wo": wlay(Wo),
        "bo": np.asarray(inp["L0_attn_bo"], f).reshape(1, 1024),
        "planes2": np.concatenate([planes_T, planes_T], 0),
        "protos": np.asarray(inp["L0_attn_protos_T"], f),
        "ltemp": np.asarray(inp["L0_attn_logit_temp"], f).reshape(1, 1),
        "negindA": negA, "negindB": negB,
        "ident": np.eye(128, dtype=f),
        "ones1": np.ones((1, 128), f),
        "onescol": np.ones((128, 1), f),
        "nh8": np.full((128, 8), -0.5, f),
        "nh1": np.full((128, 1), -0.5, f),
        "ph1": np.full((128, 1), 0.5, f),
        "mul": np.ascontiguousarray(mu.reshape(8, 128, 256).transpose(1, 0, 2).reshape(128, 2048)),
        "gam": np.ascontiguousarray(np.broadcast_to(np.asarray(inp["L0_ffn_ln_gamma"], f), (128, 1024))),
        "bet": np.ascontiguousarray(np.broadcast_to(np.asarray(inp["L0_ffn_ln_beta"], f), (128, 1024))),
        "scl": np.ascontiguousarray(np.broadcast_to(np.asarray(inp["L0_ffn_scale"], f).reshape(1, 1), (128, 1))),
        "w1l": np.ascontiguousarray(w1.reshape(E, 8, 128, 2, 128).transpose(0, 2, 1, 3, 4).reshape(E, 128, 2048)),
        "w2l": np.ascontiguousarray(w2.reshape(E, 2, 128, 2, 512).transpose(0, 2, 1, 3, 4).reshape(E, 128, 2048)),
        "b1f": np.ascontiguousarray(b1.reshape(16, 2, 128).transpose(2, 0, 1).reshape(128, 32)),
        "b2r": np.ascontiguousarray(np.repeat(np.asarray(inp["L0_ffn_b2"], f), 16, axis=0)),
    }
    return com


def kernel(**inputs):
    x = np.ascontiguousarray(np.asarray(inputs["L0_attn_x"], np.float32))
    com = _host_prep(inputs)
    nc, _ = build_program(DBG_NAMES)
    in_maps = []
    for c in range(8):
        m = dict(com)
        m["x"] = np.ascontiguousarray(x[2 * c:2 * c + 2])
        in_maps.append(m)
    res = run_bass_kernel_spmd(nc, in_maps, core_ids=list(range(8)))
    return np.concatenate([r["out"] for r in res.results], axis=0)


# version 10
# speedup vs baseline: 1.0312x; 1.0312x over previous
import numpy as np
from contextlib import ExitStack

import concourse.bass as bass
import concourse.mybir as mybir
from concourse.bass_utils import run_bass_kernel_spmd

F32 = mybir.dt.float32
BF16 = mybir.dt.bfloat16
AF = mybir.ActivationFunctionType
ALU = mybir.AluOpType
AX = mybir.AxisListType

P = 128
D = 1024
H = 16
HD = 64
NT = 16
HALF = 8
SB = 128
LTAB = 8
RB = 16
E = 16
ES = 256
HFF = 256
EPS = 1e-5

DBG_NAMES = ()
A_NQ = 8
A_CFG = dict(bs=(0, 1), halves=(0, 1), tiles=8, buckets=True, stages=99, sub9=99)


class Sm:
    def __init__(self, es, nc, name):
        self.h = es.enter_context(nc.semaphore(name))
        self.n = 0

    def inc(self, instr, k=1):
        instr.then_inc(self.h, k)
        self.n += k
        return self.n


def build_program(dbg_names=(), phases=("A", "B", "F")):
    nc = bass.Bass("TRN2", target_bir_lowering=False, detect_race_conditions=False)
    dbg_outs = {}

    x_d = nc.dram_tensor("x", [2, 2048, D], F32, kind="ExternalInput").ap()
    wq_d = nc.dram_tensor("wq", [128, 8192], F32, kind="ExternalInput").ap()
    wk_d = nc.dram_tensor("wk", [128, 8192], F32, kind="ExternalInput").ap()
    wv_d = nc.dram_tensor("wv", [128, 8192], F32, kind="ExternalInput").ap()
    wo_d = nc.dram_tensor("wo", [128, 8192], F32, kind="ExternalInput").ap()
    bo_d = nc.dram_tensor("bo", [1, D], F32, kind="ExternalInput").ap()
    planes_d = nc.dram_tensor("planes2", [128, 32], F32, kind="ExternalInput").ap()
    protos_d = nc.dram_tensor("protos", [4, 16], F32, kind="ExternalInput").ap()
    ltemp_d = nc.dram_tensor("ltemp", [1, 1], F32, kind="ExternalInput").ap()
    negA_d = nc.dram_tensor("negindA", [64, 1024], F32, kind="ExternalInput").ap()
    negB_d = nc.dram_tensor("negindB", [8, 1024], F32, kind="ExternalInput").ap()
    ident_d = nc.dram_tensor("ident", [128, 128], F32, kind="ExternalInput").ap()
    ones1_d = nc.dram_tensor("ones1", [1, 128], F32, kind="ExternalInput").ap()
    onescol_d = nc.dram_tensor("onescol", [128, 1], F32, kind="ExternalInput").ap()
    nh8_d = nc.dram_tensor("nh8", [128, 8], F32, kind="ExternalInput").ap()
    nh1_d = nc.dram_tensor("nh1", [128, 1], F32, kind="ExternalInput").ap()
    ph1_d = nc.dram_tensor("ph1", [128, 1], F32, kind="ExternalInput").ap()
    mu_d = nc.dram_tensor("mul", [128, 2048], F32, kind="ExternalInput").ap()
    gam_d = nc.dram_tensor("gam", [128, D], F32, kind="ExternalInput").ap()
    bet_d = nc.dram_tensor("bet", [128, D], F32, kind="ExternalInput").ap()
    scl_d = nc.dram_tensor("scl", [128, 1], F32, kind="ExternalInput").ap()
    w1_d = nc.dram_tensor("w1l", [E, 128, 2048], F32, kind="ExternalInput").ap()
    w2_d = nc.dram_tensor("w2l", [E, 128, 2048], F32, kind="ExternalInput").ap()
    b1_d = nc.dram_tensor("b1f", [128, 32], F32, kind="ExternalInput").ap()
    b2_d = nc.dram_tensor("b2r", [256, D], F32, kind="ExternalInput").ap()

    out_d = nc.dram_tensor("out", [2, 2048, D], F32, kind="ExternalOutput").ap()
    x2scr = nc.dram_tensor("x2scr", [2, 2048, D], F32, kind="Internal").ap()

    x_half_v = x_d.rearrange("b (hh t p) d -> b hh p t d", hh=2, t=HALF, p=P)
    x_tile_v = x_d.rearrange("b (t p) d -> b t p d", t=NT, p=P)
    x2_tile_v = x2scr.rearrange("b (t p) d -> b t p d", t=NT, p=P)
    out_tile_v = out_d.rearrange("b (t p) d -> b t p d", t=NT, p=P)

    es = ExitStack()
    with es:
        sP = Sm(es, nc, "sP")
        sV = Sm(es, nc, "sV")
        sA = Sm(es, nc, "sA")
        sG = Sm(es, nc, "sG")
        dXA = Sm(es, nc, "dXA")
        dXA2 = Sm(es, nc, "dXA2")
        dXp = [Sm(es, nc, "dX0"), Sm(es, nc, "dX1")]
        dCon = Sm(es, nc, "dCon")
        dPro = Sm(es, nc, "dPro")
        dWkv = Sm(es, nc, "dWkv")
        dWqo = Sm(es, nc, "dWqo")
        dE = [Sm(es, nc, f"dE{i}") for i in range(4)]
        dSp = [Sm(es, nc, "dS0"), Sm(es, nc, "dS1")]
        dSh = [Sm(es, nc, "dSh0"), Sm(es, nc, "dSh1")]
        dO = Sm(es, nc, "dO")
        dOp = [Sm(es, nc, "dO0"), Sm(es, nc, "dO1")]
        dDbg = Sm(es, nc, "dDbg")

        ptrb_t = es.enter_context(nc.psum_tensor("ptrb", [P, 2048], BF16))
        ptrb = ptrb_t.ap()[:, 0:1024]
        ptrbB = ptrb_t.ap()[:, 1024:2048]
        pA = es.enter_context(nc.psum_tensor("pA", [P, 512], F32))
        pB = es.enter_context(nc.psum_tensor("pB", [P, 512], F32))
        pC = es.enter_context(nc.psum_tensor("pC", [P, 512], F32))
        pD = es.enter_context(nc.psum_tensor("pD", [P, 512], F32))
        pE = es.enter_context(nc.psum_tensor("pE", [P, 1024], F32))

        identb = es.enter_context(nc.sbuf_tensor("identb", [128, 128], BF16))
        identf = es.enter_context(nc.sbuf_tensor("identf", [128, 128], F32))
        onesb = es.enter_context(nc.sbuf_tensor("onesb", [1, 128], BF16))
        ones1f = es.enter_context(nc.sbuf_tensor("ones1f", [1, 128], F32))
        onescolb = es.enter_context(nc.sbuf_tensor("onescolb", [128, 1], BF16))
        negA = es.enter_context(nc.sbuf_tensor("negA", [64, 1024], BF16))
        negB = es.enter_context(nc.sbuf_tensor("negB", [8, 1024], BF16))
        planes = es.enter_context(nc.sbuf_tensor("planes", [128, 32], BF16))
        PbigF = es.enter_context(nc.sbuf_tensor("PbigF", [32, 128], F32))
        PbigS = es.enter_context(nc.sbuf_tensor("PbigS", [128, 128], BF16))
        bo_s = es.enter_context(nc.sbuf_tensor("bo_s", [1, D], BF16))
        lt_s = es.enter_context(nc.sbuf_tensor("lt_s", [1, 8], F32))
        nh8_s = es.enter_context(nc.sbuf_tensor("nh8_s", [128, 8], F32))
        nh1_s = es.enter_context(nc.sbuf_tensor("nh1_s", [128, 1], F32))
        ph1_s = es.enter_context(nc.sbuf_tensor("ph1_s", [128, 1], F32))
        invs32 = es.enter_context(nc.sbuf_tensor("invs32", [32, 1], F32))

        block = es.enter_context(nc.Block())
        T = block.tensor
        V = block.vector
        A = block.scalar
        G = block.gpsimd
        Q = block.sync

        def dump(name, ap, sem, val):
            if name not in dbg_names:
                return
            dd = nc.dram_tensor("dbg_" + name, list(ap.shape), F32, kind="ExternalOutput").ap()
            dbg_outs[name] = dd

            def _(g):
                g.wait_ge(sem.h, val)
                dDbg.inc(g.dma_start(out=dd, in_=ap), 16)
            G(_)

        def _(g):
            g.dma_start(out=identb[:], in_=ident_d).then_inc(dCon.h, 16)
            g.dma_start(out=identf[:], in_=ident_d).then_inc(dCon.h, 16)
            g.dma_start(out=onesb[:], in_=ones1_d).then_inc(dCon.h, 16)
            g.dma_start(out=ones1f[:], in_=ones1_d).then_inc(dCon.h, 16)
            g.dma_start(out=onescolb[:], in_=onescol_d).then_inc(dCon.h, 16)
            g.dma_start(out=negA[:], in_=negA_d).then_inc(dCon.h, 16)
            g.dma_start(out=negB[:], in_=negB_d).then_inc(dCon.h, 16)
            g.dma_start(out=planes[:], in_=planes_d).then_inc(dCon.h, 16)
            g.dma_start(out=bo_s[:], in_=bo_d).then_inc(dCon.h, 16)
            g.dma_start(out=lt_s[0:1, 0:1], in_=ltemp_d).then_inc(dCon.h, 16)
            g.dma_start(out=nh8_s[:], in_=nh8_d).then_inc(dCon.h, 16)
            g.dma_start(out=nh1_s[:], in_=nh1_d).then_inc(dCon.h, 16)
            g.dma_start(out=ph1_s[:], in_=ph1_d).then_inc(dCon.h, 16)
            dCon.n += 208
        G(_)
        w_pre = dCon.n

        def _(v):
            sV.inc(v.memset(PbigF[:], 0.0))
        V(_)
        v_pz = sV.n

        def _(g):
            g.wait_ge(sV.h, v_pz)
            for l in range(LTAB):
                g.dma_start(out=PbigF[4 * l:4 * l + 4, 16 * l:16 * l + 16], in_=protos_d).then_inc(dPro.h, 16)
            dPro.n += 16 * LTAB
        G(_)
        w_pbig = dPro.n

        def _(s):
            s.wait_ge(dCon.h, w_pre)
            sA.inc(s.activation(lt_s[0:1, 1:2], lt_s[0:1, 0:1], AF.Exp))
        A(_)
        a_e = sA.n

        def _(v):
            v.wait_ge(sA.h, a_e)
            sV.inc(v.tensor_scalar(lt_s[0:1, 2:3], lt_s[0:1, 1:2], 20.0, 0.01, op0=ALU.min, op1=ALU.max))
            v.wait_ge(sV.h, sV.n)
            sV.inc(v.reciprocal(lt_s[0:1, 3:4], lt_s[0:1, 2:3]))
        V(_)
        v_invs = sV.n

        def _(t):
            t.wait_ge(sV.h, v_invs)
            sP.inc(t.matmul(pD[0:32, 0:1], ones1f[0:1, 0:32], lt_s[0:1, 3:4], start=True, stop=True))
        T(_)
        p_inv = sP.n

        def _(s):
            s.wait_ge(sP.h, p_inv)
            sA.inc(s.copy(invs32[:], pD[0:32, 0:1]))
        A(_)
        a_inv = sA.n

        def _(v):
            v.wait_ge(sA.h, a_inv)
            v.wait_ge(dPro.h, w_pbig)
            sV.inc(v.tensor_scalar(PbigS[0:32, :], PbigF[:], invs32[:], None, op0=ALU.mult))
            sV.inc(v.tensor_scalar(PbigS[64:96, :], PbigF[:], invs32[:], None, op0=ALU.mult))
        V(_)
        v_pbig = sV.n
        dump("PbigS", PbigS[0:32, :], sV, v_pbig)


        stats = es.enter_context(nc.sbuf_tensor("stats", [P, 64], F32))
        negm1 = es.enter_context(nc.sbuf_tensor("negm1", [P, 32], F32))
        r1s = es.enter_context(nc.sbuf_tensor("r1s", [P, 32], F32))

        esAB = es.enter_context(ExitStack())
        bKTa = [esAB.enter_context(nc.sbuf_tensor(f"bKTa{b}", [64, 2048], F32)) for b in range(2)]
        bVa = [esAB.enter_context(nc.sbuf_tensor(f"bVa{b}", [P, 1024], F32)) for b in range(2)]

        esA = ExitStack()
        wk_s = esA.enter_context(nc.sbuf_tensor("wk_s", [128, 8192], BF16))
        wv_s = esA.enter_context(nc.sbuf_tensor("wv_s", [128, 8192], BF16))
        x_sb = esA.enter_context(nc.sbuf_tensor("x_sb", [P, HALF, D], F32))
        xn1p = [esA.enter_context(nc.sbuf_tensor(f"xn1p{i}", [P, D], BF16)) for i in range(2)]
        xn1f_q = esA.enter_context(nc.sbuf_tensor("xn1f_q", [P, 4, D], BF16))
        Kfm_q = esA.enter_context(nc.sbuf_tensor("Kfm_q", [P, 8, 512], BF16))
        tanhp_q = esA.enter_context(nc.sbuf_tensor("tanhp_q", [P, 4, D], BF16))
        Ktm_qp = [esA.enter_context(nc.sbuf_tensor(f"Ktm_qp{i}", [P, 4, D], BF16)) for i in range(2)]
        Vtm_qp = [esA.enter_context(nc.sbuf_tensor(f"Vtm_qp{i}", [P, 4, D], BF16)) for i in range(2)]
        probs_qp = [esA.enter_context(nc.sbuf_tensor(f"probs_qp{i}", [P, 4, 2048], BF16)) for i in range(2)]
        bnbuf = esA.enter_context(nc.sbuf_tensor("bnbuf", [P, 96], F32))
        tsb = esA.enter_context(nc.sbuf_tensor("tsb", [P, 8], F32))
        nm64 = [esA.enter_context(nc.sbuf_tensor(f"nm64_{i}", [P, 64], F32)) for i in range(2)]
        Z64 = [esA.enter_context(nc.sbuf_tensor(f"Z64_{i}", [P, 64], F32)) for i in range(2)]
        R64 = [esA.enter_context(nc.sbuf_tensor(f"R64_{i}", [P, 64], F32)) for i in range(2)]
        R64b = [esA.enter_context(nc.sbuf_tensor(f"R64b_{i}", [P, 64], BF16)) for i in range(2)]
        scrA = [esA.enter_context(nc.sbuf_tensor(f"scrA{i}", [P, 1024], F32)) for i in range(2)]

        def _(g):
            g.dma_start(out=wk_s[:], in_=wk_d).then_inc(dWkv.h, 16)
            g.dma_start(out=wv_s[:], in_=wv_d).then_inc(dWkv.h, 16)
            dWkv.n += 32
        G(_)
        w_attn = dWkv.n

        trk = {
            "x_free": 0,
            "ptrb_free": 0,
            "ptrf_free": 0,
            "pAB_free_pe": 0,
            "pCD_free": 0,
            "pE_free": 0,
            "prod_free_pe": 0,
            "bk_ready": 0,
            "x2_store": 0,
        }

        dXq = [dXA, dXA2]
        atrk = {
            "pt_free": [0, 0],
            "xp_free": [0, 0],
            "xldv": {},
            "r1s": {},
            "xe": {},
            "xn": {},
            "ke": {},
            "kte": {},
            "th": {},
            "vm": {},
            "ve": {},
            "lg": {},
            "ex": {},
            "sub": {},
            "norm": {},
            "bk_adds": {},
        }
        SCB_A = [(pE[:, 0:512], pE[:, 512:1024]), (pA[:, 0:512], pB[:, 0:512])]

        def a_load_half(it_next):
            b, half = it_next // 4, (it_next % 4) // 2
            g1 = atrk["xn"].get(it_next - 2, 0)
            g2 = atrk["xn"].get(it_next - 1, 0)

            def _(q, b=b, half=half, g1=g1, g2=g2):
                if g1:
                    q.wait_ge(sV.h, g1)
                dXq[0].inc(q.dma_start(out=x_sb[:, 0:4, :], in_=x_half_v[b, half][:, 0:4, :]), 16)
                if g2:
                    q.wait_ge(sV.h, g2)
                dXq[1].inc(q.dma_start(out=x_sb[:, 4:8, :], in_=x_half_v[b, half][:, 4:8, :]), 16)
            Q(_)
            atrk["xldv"][it_next] = dXq[0].n
            atrk["xldv"][it_next + 1] = dXq[1].n

        def a_stats_q(it):
            b, qq = it // 4, it % 4
            sgate = atrk["xldv"][it]
            dX = dXq[it % 2]
            c0 = (qq % 2) * 4

            def _(v, b=b, qq=qq, c0=c0, sgate=sgate, dX=dX):
                v.wait_ge(dX.h, sgate)
                for tl in range(4):
                    o = (c0 + tl) * 12
                    sV.inc(v.bn_stats(bnbuf[:, o:o + 6], x_sb[:, c0 + tl, 0:512]))
                    sV.inc(v.bn_stats(bnbuf[:, o + 6:o + 12], x_sb[:, c0 + tl, 512:1024]))
                v.wait_ge(sV.h, sV.n)
                for tl in range(4):
                    col = b * 32 + (qq * 4 + tl) * 2
                    o = (c0 + tl) * 12
                    sV.inc(v.bn_aggr(stats[:, col:col + 2], bnbuf[:, o:o + 12]))
                v.wait_ge(sV.h, sV.n)
                t0 = b * 16 + qq * 4
                vsl = stats[:, b * 32 + qq * 8 + 1: b * 32 + qq * 8 + 8: 2]
                msl = stats[:, b * 32 + qq * 8: b * 32 + qq * 8 + 8: 2]
                sV.inc(v.tensor_scalar(tsb[:, c0:c0 + 4], vsl, EPS, None, op0=ALU.add))
                sV.inc(v.tensor_scalar(negm1[:, t0:t0 + 4], msl, -1.0, None, op0=ALU.mult))
            V(_)
            v_stats = sV.n

            def _(g, b=b, qq=qq, c0=c0, v_stats=v_stats):
                g.wait_ge(sV.h, v_stats)
                g.wait_ge(dCon.h, 208)
                t0 = b * 16 + qq * 4
                sG.inc(g.tensor_tensor(r1s[:, t0:t0 + 4], tsb[:, c0:c0 + 4], nh8_s[:, 0:4], op=ALU.pow))
            G(_)
            atrk["r1s"][it] = sG.n

        def a_xnxt(it):
            b, qq = it // 4, it % 4
            r1s_g = atrk["r1s"][it]
            x_ld = atrk["xldv"][it]
            dX = dXq[it % 2]
            vm_prev = atrk["vm"].get(it - 1, 0)
            for tl in range(4):
                par = tl % 2
                ncol = b * 16 + qq * 4 + tl
                hh_tl = (qq % 2) * 4 + tl
                xp_free = atrk["xp_free"][par]

                def _(v, ncol=ncol, hh_tl=hh_tl, par=par, r1s_g=r1s_g, x_ld=x_ld, dX=dX, xp_free=xp_free):
                    v.wait_ge(dX.h, x_ld)
                    v.wait_ge(sG.h, r1s_g)
                    if xp_free:
                        v.wait_ge(sP.h, xp_free)
                    sV.inc(v.tensor_scalar(xn1p[par][:], x_sb[:, hh_tl, :], negm1[:, ncol:ncol + 1],
                                           r1s[:, ncol:ncol + 1], op0=ALU.add, op1=ALU.mult))
                V(_)
                v_xn = sV.n

                pt = ptrb if par == 0 else ptrbB
                pt_free = atrk["pt_free"][par]

                def _(t, it=it, pt=pt, par=par, v_xn=v_xn, pt_free=pt_free):
                    t.wait_ge(sV.h, v_xn)
                    if it == 0:
                        t.wait_ge(dCon.h, 208)
                    if pt_free:
                        t.wait_ge(sA.h, pt_free)
                    for kc in range(8):
                        ii = t.transpose(pt[:, kc * 128:kc * 128 + 128], xn1p[par][:, kc * 128:kc * 128 + 128], identb[:])
                    sP.inc(ii)
                T(_)
                atrk["xp_free"][par] = sP.n
                p_xt = sP.n

                def _(s, tl=tl, pt=pt, p_xt=p_xt, vm_prev=vm_prev):
                    s.wait_ge(sP.h, p_xt)
                    if vm_prev:
                        s.wait_ge(sP.h, vm_prev)
                    sA.inc(s.copy(xn1f_q[:, tl, :], pt[:]))
                A(_)
                atrk["pt_free"][par] = sA.n
            atrk["xe"][it] = sA.n
            atrk["xn"][it] = sV.n

        def a_km(it, oc):
            bank = pC if oc % 2 == 0 else pD
            if oc < 2:
                w_free = atrk["ve"].get(it - 1, 0)
            else:
                w_free = atrk["ke"][(it, oc - 2)]
            xe_v = atrk["xe"][it]

            def _(t, it=it, oc=oc, bank=bank, w_free=w_free, xe_v=xe_v):
                t.wait_ge(sA.h, xe_v)
                if it == 0 and oc == 0:
                    t.wait_ge(dWkv.h, 32)
                if w_free:
                    t.wait_ge(sA.h, w_free)
                for kc in range(8):
                    ii = t.matmul(bank[:, 0:512],
                                  wk_s[:, kc * 1024 + oc * 128:kc * 1024 + oc * 128 + 128],
                                  xn1f_q[:, 0:4, kc * 128:kc * 128 + 128],
                                  start=(kc == 0), stop=(kc == 7), skip_group_check=True)
                sP.inc(ii)
            T(_)
            p_km = sP.n

            def _(s, oc=oc, bank=bank, p_km=p_km):
                s.wait_ge(sP.h, p_km)
                sA.inc(s.copy(Kfm_q[:, oc, :], bank[:]))
            A(_)
            atrk["ke"][(it, oc)] = sA.n

        def a_kt(it):
            ke_last = atrk["ke"][(it, 7)]
            pq = it % 2
            for tl in range(4):
                par = tl % 2
                pt = ptrb if par == 0 else ptrbB
                pt_free = atrk["pt_free"][par]

                def _(t, tl=tl, pt=pt, ke_last=ke_last, pt_free=pt_free):
                    t.wait_ge(sA.h, ke_last)
                    if pt_free:
                        t.wait_ge(sA.h, pt_free)
                    for oc in range(8):
                        ii = t.transpose(pt[:, oc * 128:oc * 128 + 128], Kfm_q[:, oc, tl * 128:tl * 128 + 128], identb[:])
                    sP.inc(ii)
                T(_)
                p_kt = sP.n

                def _(s, tl=tl, pt=pt, p_kt=p_kt, pq=pq):
                    s.wait_ge(sP.h, p_kt)
                    sA.inc(s.copy(Ktm_qp[pq][:, tl, :], pt[:]))
                A(_)
                atrk["pt_free"][par] = sA.n
            atrk["kte"][it] = sA.n

        def a_pjth(it):
            for h in range(H):
                bank = pC if h % 2 == 0 else pD
                base = 64 * (h % 2)
                oc = h // 2
                if h < 2:
                    w_free = atrk["ke"][(it, 6 + h)]
                else:
                    w_free = atrk["th"][(it, h - 2)]

                def _(t, bank=bank, base=base, oc=oc, w_free=w_free):
                    t.wait_ge(sA.h, w_free)
                    sP.inc(t.matmul(bank[base:base + 32, 0:512],
                                    planes[base:base + 64, :],
                                    Kfm_q[base:base + 64, oc, :],
                                    start=True, stop=True))
                T(_)
                p_pj = sP.n

                def _(s, bank=bank, base=base, oc=oc, p_pj=p_pj):
                    s.wait_ge(sP.h, p_pj)
                    sA.inc(s.activation(tanhp_q[base:base + 32, 0:4, oc * 128:oc * 128 + 128],
                                        bank[base:base + 32, 0:512].rearrange("p (t c) -> p t c", c=128),
                                        AF.Tanh))
                A(_)
                atrk["th"][(it, h)] = sA.n

        def a_vmve(it):
            pq = it % 2
            ve_prev = 0
            for tl in range(4):
                w_th = atrk["th"][(it, 15)] if tl == 0 else 0

                def _(t, tl=tl, w_th=w_th, ve_prev=ve_prev):
                    if w_th:
                        t.wait_ge(sA.h, w_th)
                    if ve_prev:
                        t.wait_ge(sA.h, ve_prev)
                    for dh in range(2):
                        bank = pC if dh == 0 else pD
                        for kc in range(8):
                            ii = t.matmul(bank[:, 0:512],
                                          xn1f_q[:, tl, kc * 128:kc * 128 + 128],
                                          wv_s[:, kc * 1024 + dh * 512:kc * 1024 + dh * 512 + 512],
                                          start=(kc == 0), stop=(kc == 7))
                    sP.inc(ii)
                T(_)
                p_vm = sP.n
                atrk["vm"][it] = p_vm

                def _(s, tl=tl, p_vm=p_vm, pq=pq):
                    s.wait_ge(sP.h, p_vm)
                    sA.inc(s.copy(Vtm_qp[pq][:, tl, 0:512], pC[:]))
                    sA.inc(s.copy(Vtm_qp[pq][:, tl, 512:1024], pD[:]))
                A(_)
                ve_prev = sA.n
            atrk["ve"][it] = sA.n

        def a_lg(it, j):
            tl, grp = j // 2, j % 2
            par = j % 2
            banks = SCB_A[par]
            if j >= 2:
                w_ex = atrk["sub"][(it, j - 2)]
            elif j == 0:
                w_ex = atrk["sub"].get((it - 1, 6), 0)
            else:
                w_ex = 0
            w_bk = atrk["bk_adds"].get(it - 1, 0) if j == 1 else 0
            w_th = atrk["th"][(it, 15)]
            w_pb = v_pbig if it == 0 else 0

            def _(t, tl=tl, grp=grp, banks=banks, w_ex=w_ex, w_bk=w_bk, w_th=w_th, w_pb=w_pb):
                t.wait_ge(sA.h, w_th)
                if w_ex:
                    t.wait_ge(sV.h, w_ex)
                if w_bk:
                    t.wait_ge(sV.h, w_bk)
                if w_pb:
                    t.wait_ge(sV.h, w_pb)
                for hl in range(8):
                    h = grp * 8 + hl
                    base = 64 * (h % 2)
                    bank = banks[hl % 2]
                    ii = t.matmul(bank[:, (hl // 2) * 128:(hl // 2) * 128 + 128],
                                  tanhp_q[base:base + 32, tl, (h // 2) * 128:(h // 2) * 128 + 128],
                                  PbigS[base:base + 32, :],
                                  start=(hl in (0, 1)), stop=(hl in (6, 7)),
                                  skip_group_check=True)
                sP.inc(ii)
            T(_)
            atrk["lg"][(it, j)] = sP.n

        def a_nmex(it, j):
            tl, grp = j // 2, j % 2
            par = j % 2
            banks = SCB_A[par]
            w_lg = atrk["lg"][(it, j)]

            def _(v, par=par, banks=banks, w_lg=w_lg):
                v.wait_ge(sP.h, w_lg)
                sV.inc(v.tensor_reduce(nm64[par][:, 0:32], banks[0].rearrange("p (s r) -> p s r", r=16),
                                       axis=AX.X, op=ALU.max, negate=True))
                sV.inc(v.tensor_reduce(nm64[par][:, 32:64], banks[1].rearrange("p (s r) -> p s r", r=16),
                                       axis=AX.X, op=ALU.max, negate=True))
                v.wait_ge(sV.h, sV.n)
                sV.inc(v.tensor_tensor(scrA[par][:, 0:512].rearrange("p (s r) -> p s r", r=16),
                                       banks[0].rearrange("p (s r) -> p s r", r=16),
                                       nm64[par][:, 0:32].unsqueeze(2).broadcast_to([P, 32, 16]), op=ALU.add))
                sV.inc(v.tensor_tensor(scrA[par][:, 512:1024].rearrange("p (s r) -> p s r", r=16),
                                       banks[1].rearrange("p (s r) -> p s r", r=16),
                                       nm64[par][:, 32:64].unsqueeze(2).broadcast_to([P, 32, 16]), op=ALU.add))
            V(_)
            w_sub = sV.n
            atrk["sub"][(it, j)] = w_sub

            pq = it % 2

            def _(s, tl=tl, grp=grp, par=par, w_sub=w_sub, pq=pq):
                s.wait_ge(sV.h, w_sub)
                dst = probs_qp[pq][:, tl, grp * 1024:grp * 1024 + 1024].rearrange("p (c k) -> p c k", k=256)
                sA.inc(s.activation(dst[:, :, 0:128],
                                    scrA[par][:, 0:512].rearrange("p (c s) -> p c s", s=128), AF.Exp))
                sA.inc(s.activation(dst[:, :, 128:256],
                                    scrA[par][:, 512:1024].rearrange("p (c s) -> p c s", s=128), AF.Exp))
            A(_)
            atrk["ex"][(it, j)] = sA.n

        def a_norm(it, j):
            tl, grp = j // 2, j % 2
            par = j % 2
            pq = it % 2
            w_ex = atrk["ex"][(it, j)]

            def _(v, tl=tl, grp=grp, par=par, pq=pq, w_ex=w_ex):
                v.wait_ge(sA.h, w_ex)
                sl = probs_qp[pq][:, tl, grp * 1024:grp * 1024 + 1024].rearrange("p (s r) -> p s r", r=16)
                sV.inc(v.tensor_reduce(Z64[par][:], sl, axis=AX.X, op=ALU.add))
                v.wait_ge(sV.h, sV.n)
                sV.inc(v.reciprocal(R64[par][:], Z64[par][:]))
                v.wait_ge(sV.h, sV.n)
                sV.inc(v.tensor_copy(R64b[par][:], R64[par][:]))
                v.wait_ge(sV.h, sV.n)
                sV.inc(v.tensor_tensor(sl, sl, R64b[par][:].unsqueeze(2).broadcast_to([P, 64, 16]), op=ALU.mult))
            V(_)
            atrk["norm"][(it, j)] = sV.n

        def a_bk(it):
            b, qq = it // 4, it % 4
            pq = it % 2
            w_norm = atrk["norm"][(it, 7)]
            w_kte = atrk["kte"][it]
            w_ve = atrk["ve"][it]
            last_add = [0, 0]
            for h in range(H):
                bank = pA if h % 2 == 0 else pB
                prev_add = last_add[h % 2]

                def _(t, h=h, bank=bank, pq=pq, w_norm=w_norm, w_kte=w_kte, w_ve=w_ve, prev_add=prev_add):
                    if h < 2:
                        t.wait_ge(sV.h, w_norm)
                        t.wait_ge(sA.h, w_kte)
                        t.wait_ge(sA.h, w_ve)
                    if prev_add:
                        t.wait_ge(sV.h, prev_add)
                    for tt in range(4):
                        t.matmul(bank[0:64, 0:128], Ktm_qp[pq][:, tt, h * 64:h * 64 + 64],
                                 probs_qp[pq][:, tt, h * 128:h * 128 + 128],
                                 start=(tt == 0), stop=(tt == 3), skip_group_check=True)
                    for tt in range(4):
                        ii = t.matmul(bank[0:128, 128:192], probs_qp[pq][:, tt, h * 128:h * 128 + 128],
                                      Vtm_qp[pq][:, tt, h * 64:h * 64 + 64],
                                      start=(tt == 0), stop=(tt == 3), skip_group_check=True)
                    sP.inc(ii)
                T(_)
                p_bk = sP.n

                def _(v, b=b, qq=qq, h=h, bank=bank, p_bk=p_bk):
                    v.wait_ge(sP.h, p_bk)
                    if qq == 0:
                        sV.inc(v.tensor_copy(bKTa[b][:, h * 128:h * 128 + 128], bank[0:64, 0:128]))
                        sV.inc(v.tensor_copy(bVa[b][:, h * 64:h * 64 + 64], bank[0:128, 128:192]))
                    else:
                        sV.inc(v.tensor_tensor(bKTa[b][:, h * 128:h * 128 + 128], bKTa[b][:, h * 128:h * 128 + 128],
                                               bank[0:64, 0:128], op=ALU.add))
                        sV.inc(v.tensor_tensor(bVa[b][:, h * 64:h * 64 + 64], bVa[b][:, h * 64:h * 64 + 64],
                                               bank[0:128, 128:192], op=ALU.add))
                V(_)
                last_add[h % 2] = sV.n
            atrk["bk_adds"][it] = sV.n

        def emit_A_all():
            NQ = A_NQ
            a_load_half(0)
            a_stats_q(0)
            a_xnxt(0)
            for it in range(NQ):
                for j in range(10):
                    if j < 8:
                        a_km(it, j)
                    if it >= 1:
                        if j < 8:
                            a_lg(it - 1, j)
                        if 1 <= j <= 8:
                            a_nmex(it - 1, j - 1)
                        if 2 <= j <= 9:
                            a_norm(it - 1, j - 2)
                a_kt(it)
                a_pjth(it)
                a_vmve(it)
                if it + 1 < NQ:
                    if (it + 1) % 2 == 0:
                        a_load_half(it + 1)
                    a_stats_q(it + 1)
                    a_xnxt(it + 1)
                if it >= 1:
                    a_bk(it - 1)
                    if it - 1 == 3:
                        trk["bk_evac0"] = atrk["bk_adds"][3]
            for j in range(10):
                if j < 8:
                    a_lg(NQ - 1, j)
                if 1 <= j <= 8:
                    a_nmex(NQ - 1, j - 1)
                if 2 <= j <= 9:
                    a_norm(NQ - 1, j - 2)
            a_bk(NQ - 1)
            trk["bk_evac1"] = atrk["bk_adds"][NQ - 1]
            trk.setdefault("bk_evac0", trk["bk_evac1"])
            dump("bKT_b0", bKTa[0][:], sV, trk["bk_evac0"])
            dump("bV_b0", bVa[0][:], sV, trk["bk_evac0"])
            dump("bKT_b1", bKTa[1][:], sV, trk["bk_evac1"])
            dump("bV_b1", bVa[1][:], sV, trk["bk_evac1"])
            dump("xn1f_q0", xn1f_q[:], sA, atrk["xe"][0])
            dump("Kfm_q0", Kfm_q[:], sA, atrk["ke"][(0, 7)])
            dump("Ktm_q0", Ktm_qp[0][:], sA, atrk["kte"][0])
            dump("Vtm_q0", Vtm_qp[0][:], sA, atrk["ve"][0])
            dump("tanhp_q0a", tanhp_q[0:32, :, :], sA, atrk["th"][(0, 15)])
            dump("tanhp_q0b", tanhp_q[64:96, :, :], sA, atrk["th"][(0, 15)])
            dump("probs_q0", probs_qp[0][:], sV, atrk["norm"][(0, 7)])

        def emit_B(b):
            def _(v, b=b):
                v.wait_ge(sV.h, trk[f"bk_evac{b}"])
                sV.inc(v.tensor_copy(bKTb[b][0:64, :], bKTa[b][:]))
                sV.inc(v.tensor_copy(bKTb[b][64:128, :], bKTa[b][:]))
                sV.inc(v.tensor_copy(bVb[b][:], bVa[b][:]))
            V(_)
            bk_ready = sV.n

            for tt in range(NT):
                par = tt % 2
                xin_free = trk.get(f"xinfree_{par}", 0)

                def _(q, b=b, tt=tt, par=par, xin_free=xin_free):
                    if xin_free:
                        q.wait_ge(sV.h, xin_free)
                    dXp[par].inc(q.dma_start(out=xin[par][:], in_=x_tile_v[b, tt]), 16)
                Q(_)
                x_ld = dXp[par].n

                p_trs_prev = trk.get(f"b0trs_{par}", 0)

                def _(v, b=b, tt=tt, par=par, x_ld=x_ld, p_trs_prev=p_trs_prev):
                    v.wait_ge(dXp[par].h, x_ld)
                    if p_trs_prev:
                        v.wait_ge(sP.h, p_trs_prev)
                    ncol = b * 16 + tt
                    sV.inc(v.tensor_scalar(xnB[par][:], xin[par][:], negm1[:, ncol:ncol + 1],
                                           r1s[:, ncol:ncol + 1], op0=ALU.add, op1=ALU.mult))
                V(_)
                v_xn1 = sV.n
                trk[f"xinfree_{par}"] = v_xn1

                pt = ptrb if par == 0 else ptrbB
                a_evac_prev = trk.get(f"b0ev_{par}", 0)

                def _(t, par=par, pt=pt, v_xn1=v_xn1, a_evac_prev=a_evac_prev):
                    t.wait_ge(sV.h, v_xn1)
                    if a_evac_prev:
                        t.wait_ge(sA.h, a_evac_prev)
                    for kc in range(8):
                        ii = t.transpose(pt[:, kc * 128:kc * 128 + 128], xnB[par][:, kc * 128:kc * 128 + 128], identb[:])
                    sP.inc(ii)
                T(_)
                trk[f"b0trs_{par}"] = sP.n
                p_trs = sP.n

                def _(s, tt=tt, pt=pt, p_trs=p_trs):
                    s.wait_ge(sP.h, p_trs)
                    sA.inc(s.copy(xn1f_all[:, tt, :], pt[:]))
                A(_)
                trk[f"b0ev_{par}"] = sA.n
            a_xn1f_all = sA.n

            qb = [pA, pB, pC, pD]
            prev_evac = [0, 0, 0, 0]
            for oc in range(8):
                def _(t, oc=oc, a_xn1f_all=a_xn1f_all, pe=tuple(prev_evac)):
                    t.wait_ge(sA.h, a_xn1f_all)
                    t.wait_ge(dWqo.h, 32)
                    for tc in range(4):
                        if pe[tc]:
                            t.wait_ge(sA.h, pe[tc])
                    for kc in range(8):
                        for tc in range(4):
                            ii = t.matmul(qb[tc][:, 0:512],
                                          wq_s[:, kc * 1024 + oc * 128:kc * 1024 + oc * 128 + 128],
                                          xn1f_all[:, tc * 4:tc * 4 + 4, kc * 128:kc * 128 + 128],
                                          start=(kc == 0), stop=(kc == 7), skip_group_check=True)
                    sP.inc(ii)
                T(_)
                p_q = sP.n

                for tc in range(4):
                    def _(s, oc=oc, tc=tc, p_q=p_q):
                        s.wait_ge(sP.h, p_q)
                        sA.inc(s.copy(Q_all[:, tc * 4:tc * 4 + 4, oc * 128:oc * 128 + 128],
                                      qb[tc][:, 0:512].rearrange("p (t c) -> p t c", c=128)))
                    A(_)
                    prev_evac[tc] = sA.n
            a_q_all = sA.n

            SCB = [(pE[:, 0:512], pE[:, 512:1024]), (pA[:, 0:512], pB[:, 0:512])]
            POB = [pC, pD]
            PTB = [ptrb, ptrbB]
            nsl = NT * 2
            st_sc = [0] * nsl
            st_ex = [0] * nsl
            st_at = [0] * nsl
            st_ae = [0] * nsl
            st_oe = [0] * nsl
            st_nm = [0] * nsl
            st_ct = [0] * nsl
            st_bm = [0] * nsl

            def SC(i):
                tt, grp = divmod(i, 2)
                par = i % 2
                banks = SCB[par]
                w_ex = st_ex[i - 2] if i >= 2 else 0
                w_oe = st_oe[i - 2] if i >= 2 else 0

                def _(t, b=b, tt=tt, grp=grp, par=par, banks=banks, w_ex=w_ex, w_oe=w_oe,
                      a_q_all=a_q_all, bk_ready=bk_ready):
                    t.wait_ge(sA.h, a_q_all)
                    t.wait_ge(sV.h, bk_ready)
                    if w_ex:
                        t.wait_ge(sA.h, w_ex)
                    if w_oe:
                        t.wait_ge(sA.h, w_oe)
                    for hl in range(8):
                        h = grp * 8 + hl
                        base = 64 * (h % 2)
                        bank = banks[hl % 2]
                        ii = t.matmul(bank[:, (hl // 2) * 128:(hl // 2) * 128 + 128],
                                      Q_all[base:base + 64, tt, (h // 2) * 128:(h // 2) * 128 + 128],
                                      bKTb[b][base:base + 64, h * 128:h * 128 + 128],
                                      start=(hl in (0, 1)), stop=(hl in (6, 7)),
                                      skip_group_check=True)
                    sP.inc(ii)
                T(_)
                st_sc[i] = sP.n

            def NM(i):
                par = i % 2
                banks = SCB[par]

                def _(v, par=par, banks=banks, w=st_sc[i]):
                    v.wait_ge(sP.h, w)
                    v.tensor_reduce(nmax2[par][:, 0:4], banks[0].rearrange("p (h s) -> p h s", s=128),
                                    axis=AX.X, op=ALU.max, negate=False)
                    sV.inc(v.tensor_reduce(nmax2[par][:, 4:8], banks[1].rearrange("p (h s) -> p h s", s=128),
                                           axis=AX.X, op=ALU.max, negate=False))
                V(_)
                st_nm[i] = sV.n

            def NTCEBM(i):
                par = i % 2
                banks = SCB[par]
                w_oe = st_oe[i - 2] if i >= 2 else 0

                def _(t, par=par, w=st_nm[i], w_oe=w_oe):
                    t.wait_ge(sV.h, w)
                    if w_oe:
                        t.wait_ge(sA.h, w_oe)
                    sP.inc(t.transpose(POB[par][0:8, 384:512], nmax2[par][:], identf[:]))
                T(_)
                p_nt = sP.n

                def _(s, par=par, p_nt=p_nt):
                    s.wait_ge(sP.h, p_nt)
                    sA.inc(s.copy(cTB2[par][:], POB[par][0:8, 384:512]))
                A(_)
                st_ct[i] = sA.n

                def _(t, par=par, banks=banks, a_ct=st_ct[i]):
                    t.wait_ge(sA.h, a_ct)
                    for j in range(2):
                        ii = t.matmul(banks[j][:, 0:512], cTB2[par][:],
                                      negB[:, j * 512:j * 512 + 512],
                                      start=False, stop=True, skip_group_check=True)
                    sP.inc(ii)
                T(_)
                st_bm[i] = sP.n

            def EXZS(i):
                par = i % 2
                banks = SCB[par]

                def _(s, par=par, banks=banks, w=st_bm[i]):
                    s.wait_ge(sP.h, w)
                    s.activation(attn2[par][:, 0:512], banks[0], AF.Exp, scale=0.125)
                    sA.inc(s.activation(attn2[par][:, 512:1024], banks[1], AF.Exp, scale=0.125))
                A(_)
                st_ex[i] = sA.n

                def _(v, par=par, w=st_ex[i]):
                    v.wait_ge(sA.h, w)
                    sV.inc(v.tensor_reduce(ZB2[par][:], attn2[par][:].rearrange("p (h s) -> p h s", s=128),
                                           axis=AX.X, op=ALU.add))
                    v.wait_ge(sV.h, sV.n)
                    sV.inc(v.reciprocal(RB2[par][:], ZB2[par][:]))
                    v.wait_ge(sV.h, sV.n)
                    sV.inc(v.tensor_copy(RB2b[par][:], RB2[par][:]))
                    v.wait_ge(sV.h, sV.n)
                    sV.inc(v.tensor_tensor(attn2[par][:].rearrange("p (h s) -> p h s", s=128),
                                           attn2[par][:].rearrange("p (h s) -> p h s", s=128),
                                           RB2b[par][:].unsqueeze(2).broadcast_to([P, 8, 128]), op=ALU.mult))
                V(_)
                st_at[i] = sV.n

            def ATOM(i):
                tt, grp = divmod(i, 2)
                par = i % 2
                w_ae = st_ae[i - 2] if i >= 2 else 0

                def _(t, par=par, w=st_at[i], w_ae=w_ae):
                    t.wait_ge(sV.h, w)
                    if w_ae:
                        t.wait_ge(sA.h, w_ae)
                    for hl in range(8):
                        ii = t.transpose(PTB[par][:, hl * 128:hl * 128 + 128],
                                         attn2[par][:, hl * 128:hl * 128 + 128], identb[:])
                    sP.inc(ii)
                T(_)
                p_at = sP.n

                def _(s, par=par, p_at=p_at):
                    s.wait_ge(sP.h, p_at)
                    sA.inc(s.copy(expT2[par][:], PTB[par][:]))
                A(_)
                st_ae[i] = sA.n

                def _(t, b=b, grp=grp, par=par, a_ae=st_ae[i], w_ct=st_ct[i]):
                    t.wait_ge(sA.h, a_ae)
                    for hl in range(8):
                        h = grp * 8 + hl
                        base = 64 * (h % 2)
                        cs = (h // 2) % 4
                        ec = (hl // 2) * 128 + (hl % 2) * 512
                        ii = t.matmul(POB[par][base:base + 64, cs * 128:cs * 128 + 128],
                                      bVb[b][:, h * 64:h * 64 + 64],
                                      expT2[par][:, ec:ec + 128],
                                      start=(hl in (0, 1)), stop=(hl in (6, 7)), skip_group_check=True)
                    sP.inc(ii)
                T(_)
                p_om = sP.n

                def _(s, tt=tt, grp=grp, par=par, p_om=p_om):
                    s.wait_ge(sP.h, p_om)
                    sA.inc(s.copy(aout_all[:, tt, grp * 512:grp * 512 + 512], POB[par][:, 0:512]))
                A(_)
                st_oe[i] = sA.n

            for i in range(nsl + 3):
                if 0 <= i - 2 < nsl:
                    EXZS(i - 2)
                if 0 <= i - 3 < nsl:
                    ATOM(i - 3)
                if 0 <= i - 1 < nsl:
                    NM(i - 1)
                    NTCEBM(i - 1)
                if i < nsl:
                    SC(i)
            a_aout_all = sA.n

            wob = [(pE[:, 0:512], pE[:, 512:1024]), (pA[:, 0:512], pB[:, 0:512])]
            for tt in range(NT):
                par = tt % 2
                banks = wob[par]
                w_prev = trk.get(f"b3free_{par}", 0)

                def _(t, tt=tt, banks=banks, w_prev=w_prev, a_aout_all=a_aout_all):
                    t.wait_ge(sA.h, a_aout_all)
                    if w_prev:
                        t.wait_ge(sV.h, w_prev)
                    for dh in range(2):
                        for kc in range(8):
                            t.matmul(banks[dh], aout_all[:, tt, kc * 128:kc * 128 + 128],
                                     wo_s[:, kc * 1024 + dh * 512:kc * 1024 + dh * 512 + 512],
                                     start=(kc == 0), stop=False, skip_group_check=True)
                        ii = t.matmul(banks[dh], onesb[:], bo_s[0:1, dh * 512:dh * 512 + 512],
                                      start=False, stop=True, skip_group_check=True)
                    sP.inc(ii)
                T(_)
                p_wo = sP.n

                def _(v, par=par, banks=banks, p_wo=p_wo):
                    v.wait_ge(sP.h, p_wo)
                    sV.inc(v.bn_stats(bnb2[par][:, 0:6], banks[0]))
                    sV.inc(v.bn_stats(bnb2[par][:, 6:12], banks[1]))
                    v.wait_ge(sV.h, sV.n)
                    sV.inc(v.bn_aggr(mv2[par][:], bnb2[par][:]))
                    v.wait_ge(sV.h, sV.n)
                    sV.inc(v.tensor_scalar(sc2[par][:, 0:1], mv2[par][:, 1:2], EPS, None, op0=ALU.add))
                V(_)
                v_u2 = sV.n

                def _(g, par=par, v_u2=v_u2):
                    g.wait_ge(sV.h, v_u2)
                    sG.inc(g.tensor_tensor(sc2[par][:, 3:4], sc2[par][:, 0:1], nh1_s[:], op=ALU.pow))
                G(_)
                g_r2 = sG.n

                xr_free = max(trk.get(f"xr_free_{par}", 0), trk.get(f"xinfree_{par}", 0))

                def _(q, b=b, tt=tt, par=par, xr_free=xr_free):
                    if xr_free:
                        q.wait_ge(sV.h, xr_free)
                    dXp[par].inc(q.dma_start(out=xin[par][:], in_=x_tile_v[b, tt]), 16)
                Q(_)
                xr_ld = dXp[par].n

                x2_free = trk.get(f"x2free_{par}", 0)

                def _(v, b=b, par=par, banks=banks, g_r2=g_r2, x2_free=x2_free, xr_ld=xr_ld):
                    v.wait_ge(dXp[par].h, xr_ld)
                    v.wait_ge(sG.h, g_r2)
                    if x2_free:
                        v.wait_ge(dSp[par].h, x2_free)
                    sV.inc(v.tensor_scalar(sc2[par][:, 1:2], mv2[par][:, 0:1], sc2[par][:, 3:4], -1.0,
                                           op0=ALU.mult, op1=ALU.mult))
                    v.wait_ge(sV.h, sV.n)
                    sV.inc(v.tensor_scalar(x2t[par][:, 0:512], banks[0], sc2[par][:, 3:4], sc2[par][:, 1:2],
                                           op0=ALU.mult, op1=ALU.add))
                    sV.inc(v.tensor_scalar(x2t[par][:, 512:1024], banks[1], sc2[par][:, 3:4], sc2[par][:, 1:2],
                                           op0=ALU.mult, op1=ALU.add))
                    v.wait_ge(sV.h, sV.n)
                    sV.inc(v.tensor_tensor(x2t[par][:], x2t[par][:], xin[par][:], op=ALU.add))
                V(_)
                v_x2 = sV.n
                trk[f"b3free_{par}"] = v_x2
                trk[f"xr_free_{par}"] = v_x2
                trk[f"xinfree_{par}"] = v_x2

                def _(q, b=b, tt=tt, par=par, v_x2=v_x2):
                    q.wait_ge(sV.h, v_x2)
                    dSp[par].inc(q.dma_start(out=x2_tile_v[b, tt], in_=x2t[par][:]), 16)
                Q(_)
                trk[f"x2free_{par}"] = dSp[par].n


        def barrier():
            vals = [(sP, sP.n), (sV, sV.n), (sA, sA.n), (sG, sG.n),
                    (dXA, dXA.n), (dXA2, dXA2.n), (dXp[0], dXp[0].n), (dXp[1], dXp[1].n),
                    (dCon, dCon.n), (dPro, dPro.n), (dWkv, dWkv.n), (dWqo, dWqo.n),
                    (dSp[0], dSp[0].n), (dSp[1], dSp[1].n)]

            def _(e):
                for s, vv in vals:
                    if vv:
                        e.wait_ge(s.h, vv)
            for body in (T, V, A, G, Q):
                body(_)

        if "A" in phases:
            emit_A_all()
        barrier()
        esA.close()

        esB = ExitStack()
        wq_s = esB.enter_context(nc.sbuf_tensor("wq_s", [128, 8192], BF16))
        wo_s = esB.enter_context(nc.sbuf_tensor("wo_s", [128, 8192], BF16))

        def _(g):
            g.dma_start(out=wq_s[:], in_=wq_d).then_inc(dWqo.h, 16)
            g.dma_start(out=wo_s[:], in_=wo_d).then_inc(dWqo.h, 16)
            dWqo.n += 32
        G(_)
        xn1f_all = esB.enter_context(nc.sbuf_tensor("xn1f_all", [P, NT, D], BF16))
        Q_all = esB.enter_context(nc.sbuf_tensor("Q_all", [P, NT, D], BF16))
        aout_all = esB.enter_context(nc.sbuf_tensor("aout_all", [P, NT, D], BF16))
        xnB = [esB.enter_context(nc.sbuf_tensor(f"xnB{i}", [P, D], BF16)) for i in range(2)]
        nmax2 = [esB.enter_context(nc.sbuf_tensor(f"nmax2_{i}", [P, 8], F32)) for i in range(2)]
        cTB2 = [esB.enter_context(nc.sbuf_tensor(f"cTB2_{i}", [8, 128], BF16)) for i in range(2)]
        ZB2 = [esB.enter_context(nc.sbuf_tensor(f"ZB2_{i}", [P, 8], F32)) for i in range(2)]
        RB2 = [esB.enter_context(nc.sbuf_tensor(f"RB2_{i}", [P, 8], F32)) for i in range(2)]
        RB2b = [esB.enter_context(nc.sbuf_tensor(f"RB2b_{i}", [P, 8], BF16)) for i in range(2)]
        attn2 = [esB.enter_context(nc.sbuf_tensor(f"attn2_{i}", [P, 1024], BF16)) for i in range(2)]
        expT2 = [esB.enter_context(nc.sbuf_tensor(f"expT2_{i}", [P, 1024], BF16)) for i in range(2)]
        bnb2 = [esB.enter_context(nc.sbuf_tensor(f"bnb2_{i}", [P, 12], F32)) for i in range(2)]
        mv2 = [esB.enter_context(nc.sbuf_tensor(f"mv2_{i}", [P, 2], F32)) for i in range(2)]
        sc2 = [esB.enter_context(nc.sbuf_tensor(f"sc2_{i}", [P, 4], F32)) for i in range(2)]
        bKTb = [esB.enter_context(nc.sbuf_tensor(f"bKTb{b}", [P, 2048], BF16)) for b in range(2)]
        bVb = [esB.enter_context(nc.sbuf_tensor(f"bVb{b}", [P, 1024], BF16)) for b in range(2)]
        xin = [esB.enter_context(nc.sbuf_tensor(f"xin{i}", [P, D], F32)) for i in range(2)]
        x2t = [esB.enter_context(nc.sbuf_tensor(f"x2t{i}", [P, D], F32)) for i in range(2)]

        if "B" in phases:
            emit_B(0)
            emit_B(1)
        barrier()
        esB.close()
        esAB.close()

        if "F" in phases:
            dFc = Sm(es, nc, "dFc")
            esF = ExitStack()
            muu = esF.enter_context(nc.sbuf_tensor("muu", [P, 2048], BF16))
            gam_s = esF.enter_context(nc.sbuf_tensor("gam_s", [P, D], F32))
            bet_s = esF.enter_context(nc.sbuf_tensor("bet_s", [P, D], F32))
            scl_s = esF.enter_context(nc.sbuf_tensor("scl_s", [P, 1], F32))
            b1f_s = esF.enter_context(nc.sbuf_tensor("b1f_s", [P, 32], F32))
            b2r_s = esF.enter_context(nc.sbuf_tensor("b2r_s", [P, 2, D], BF16))
            esFpre = ExitStack()
            mu_sb = esFpre.enter_context(nc.sbuf_tensor("mu_sb", [P, 2048], F32))
            musq = esFpre.enter_context(nc.sbuf_tensor("musq", [P, 2048], BF16))
            ssmu = esFpre.enter_context(nc.sbuf_tensor("ssmu", [1, 1024], F32))
            rmub = esFpre.enter_context(nc.sbuf_tensor("rmub", [P, 256], F32))
            def _(g):
                g.dma_start(out=mu_sb[:], in_=mu_d).then_inc(dFc.h, 16)
                g.dma_start(out=gam_s[:], in_=gam_d).then_inc(dFc.h, 16)
                g.dma_start(out=bet_s[:], in_=bet_d).then_inc(dFc.h, 16)
                g.dma_start(out=scl_s[:], in_=scl_d).then_inc(dFc.h, 16)
                g.dma_start(out=b1f_s[:], in_=b1_d).then_inc(dFc.h, 16)
                g.dma_start(out=b2r_s[:], in_=b2_d.rearrange("(c p) d -> p c d", p=P)).then_inc(dFc.h, 16)
                dFc.n += 96
            G(_)
            w_fc = dFc.n

            def _(s):
                s.wait_ge(dFc.h, w_fc)
                sA.inc(s.activation(musq[:], mu_sb[:], AF.Square))
            A(_)
            a_musq = sA.n

            def _(t):
                t.wait_ge(sA.h, a_musq)
                for kc in range(8):
                    ii = t.matmul(pD[0:1, 0:256], onescolb[:], musq[:, kc * 256:kc * 256 + 256],
                                  start=(kc == 0), stop=(kc == 7))
                sP.inc(ii)
            T(_)
            p_mu = sP.n

            def _(s):
                s.wait_ge(sP.h, p_mu)
                sA.inc(s.copy(ssmu[0:1, 0:256], pD[0:1, 0:256]))
                s.wait_ge(sA.h, sA.n)
                sA.inc(s.activation(ssmu[0:1, 256:512], ssmu[0:1, 0:256], AF.Ln))
                s.wait_ge(sA.h, sA.n)
                sA.inc(s.activation(ssmu[0:1, 512:768], ssmu[0:1, 256:512], AF.Exp, scale=0.5))
            A(_)
            a_nmu = sA.n

            def _(v):
                v.wait_ge(sA.h, a_nmu)
                sV.inc(v.tensor_scalar(ssmu[0:1, 768:1024], ssmu[0:1, 512:768], 1e-12, None, op0=ALU.max))
                v.wait_ge(sV.h, sV.n)
                sV.inc(v.reciprocal(ssmu[0:1, 0:256], ssmu[0:1, 768:1024]))
            V(_)
            v_rmu = sV.n

            def _(t):
                t.wait_ge(sV.h, v_rmu)
                sP.inc(t.matmul(pD[0:128, 0:256], ones1f[:], ssmu[0:1, 0:256], start=True, stop=True))
            T(_)
            p_rmub = sP.n

            def _(s):
                s.wait_ge(sP.h, p_rmub)
                sA.inc(s.copy(rmub[:], pD[0:128, 0:256]))
            A(_)
            a_rmub = sA.n

            def _(v):
                v.wait_ge(sA.h, a_rmub)
                for kc in range(8):
                    sV.inc(v.tensor_tensor(muu[:, kc * 256:kc * 256 + 256], mu_sb[:, kc * 256:kc * 256 + 256],
                                           rmub[:], op=ALU.mult))
            V(_)
            v_muu = sV.n

            esFpre.close()
            xnn_sb = esF.enter_context(nc.sbuf_tensor("xnn_sb", [P, NT, D], BF16))
            lgsm = esF.enter_context(nc.sbuf_tensor("lgsm", [P, 2, 2048], BF16))
            comb = esF.enter_context(nc.sbuf_tensor("comb", [P, 2, NT, 256], BF16))
            si_sb = esF.enter_context(nc.sbuf_tensor("si_sb", [P, 2, 8, 256], BF16))
            so_stg = esF.enter_context(nc.sbuf_tensor("so_stg", [P, 2, 2, D], BF16))
            so_bf = esF.enter_context(nc.sbuf_tensor("so_bf", [P, 2, 2, D], BF16))
            w1buf = [esF.enter_context(nc.sbuf_tensor(f"w1b{i}", [P, 2048], BF16)) for i in range(4)]
            w2buf = [esF.enter_context(nc.sbuf_tensor(f"w2b{i}", [P, 2048], BF16)) for i in range(4)]
            so_st = [esF.enter_context(nc.sbuf_tensor(f"sost{i}", [32, 1024], F32)) for i in range(2)]
            h_bf = esF.enter_context(nc.sbuf_tensor("h_bf", [P, 64], BF16))
            x2f = [esF.enter_context(nc.sbuf_tensor(f"x2f{i}", [P, D], F32)) for i in range(2)]
            t1f = esF.enter_context(nc.sbuf_tensor("t1f", [P, D], F32))
            xu_tp = [esF.enter_context(nc.sbuf_tensor(f"xu_tp{i}", [P, D], BF16)) for i in range(2)]
            xufp = [esF.enter_context(nc.sbuf_tensor(f"xufp{i}", [P, D], BF16)) for i in range(2)]
            lg_tp = [esF.enter_context(nc.sbuf_tensor(f"lg_tp{i}", [P, 256], F32)) for i in range(2)]
            expcp = [esF.enter_context(nc.sbuf_tensor(f"expcp{i}", [P, 256], F32)) for i in range(2)]
            scT = esF.enter_context(nc.sbuf_tensor("scT", [P, NT, 16], F32))
            mvT = esF.enter_context(nc.sbuf_tensor("mvT", [P, NT, 4], F32))
            bnF = esF.enter_context(nc.sbuf_tensor("bnF", [P, 12], F32))
            mvF = esF.enter_context(nc.sbuf_tensor("mvF", [P, 2], F32))
            scF = esF.enter_context(nc.sbuf_tensor("scF", [P, 16], F32))
            scD = esF.enter_context(nc.sbuf_tensor("scD", [P, 4], F32))
            scC = esF.enter_context(nc.sbuf_tensor("scC", [P, 4], F32))
            dspb = esF.enter_context(nc.sbuf_tensor("dspb", [P, 2048], BF16))
            esFd = ExitStack()
            dtm = esFd.enter_context(nc.sbuf_tensor("dtm", [P, NT, 256], BF16))

            ftrk = {"ptrb": 0, "ptrf": 0, "pA": 0, "si": {}, "comb": {}}

            fst = ftrk.setdefault("fst", {})

            def fget(name, b, t):
                if t >= 0:
                    return fst.get((name, b, t), 0)
                return fst.get((name, b - 1, NT + t), 0)

            def fS1(b, t):
                par = t % 2
                xfree = ftrk.get(f"x2f{par}", 0)

                def _(q, b=b, t=t, par=par, xfree=xfree):
                    if xfree:
                        q.wait_ge(sV.h, xfree)
                    if b == 0 and t < 2:
                        q.wait_ge(sV.h, v_muu)
                    dXp[par].inc(q.dma_start(out=x2f[par][:], in_=x2_tile_v[b, t]), 16)
                Q(_)
                x_ld = dXp[par].n

                def _(v, t=t, par=par, x_ld=x_ld):
                    v.wait_ge(dXp[par].h, x_ld)
                    sV.inc(v.bn_stats(bnF[:, 0:6], x2f[par][:, 0:512]))
                    sV.inc(v.bn_stats(bnF[:, 6:12], x2f[par][:, 512:1024]))
                    v.wait_ge(sV.h, sV.n)
                    sV.inc(v.bn_aggr(mvT[:, t, 0:2], bnF[:]))
                    v.wait_ge(sV.h, sV.n)
                    sV.inc(v.tensor_scalar(scT[:, t, 0:1], mvT[:, t, 1:2], EPS, None, op0=ALU.add))
                V(_)
                v_u3 = sV.n

                def _(g, t=t, v_u3=v_u3):
                    g.wait_ge(sV.h, v_u3)
                    g.wait_ge(dFc.h, w_fc)
                    sG.inc(g.tensor_tensor(scT[:, t, 1:2], scT[:, t, 0:1], nh1_s[:], op=ALU.pow))
                G(_)
                fst[("r3", b, t)] = sG.n

            def fS3(b, t):
                w = fst[("r3", b, t)]

                def _(v, t=t, w=w):
                    v.wait_ge(sG.h, w)
                    sV.inc(v.tensor_scalar(scT[:, t, 2:3], mvT[:, t, 1:2], scT[:, t, 1:2], scT[:, t, 1:2],
                                           op0=ALU.mult, op1=ALU.mult))
                    v.wait_ge(sV.h, sV.n)
                    sV.inc(v.tensor_scalar(scT[:, t, 3:4], scT[:, t, 2:3], EPS, None, op0=ALU.add))
                V(_)
                vv = sV.n

                def _(g, t=t, vv=vv):
                    g.wait_ge(sV.h, vv)
                    sG.inc(g.tensor_tensor(scT[:, t, 4:5], scT[:, t, 3:4], nh1_s[:], op=ALU.pow))
                G(_)
                fst[("r4", b, t)] = sG.n

            def fS5(b, t):
                par = t % 2
                w_r4 = fst[("r4", b, t)]
                si_mm = ftrk.get("si_mm_done", 0) if t == 0 else 0

                def _(v, b=b, t=t, par=par, w_r4=w_r4, si_mm=si_mm):
                    v.wait_ge(sG.h, w_r4)
                    if si_mm:
                        v.wait_ge(sP.h, si_mm)
                    sV.inc(v.tensor_tensor(scT[:, t, 5:6], scT[:, t, 1:2], scT[:, t, 4:5], op=ALU.mult))
                    v.wait_ge(sV.h, sV.n)
                    sV.inc(v.tensor_scalar(scT[:, t, 6:7], mvT[:, t, 0:1], scT[:, t, 5:6], -1.0,
                                           op0=ALU.mult, op1=ALU.mult))
                    v.wait_ge(sV.h, sV.n)
                    sV.inc(v.tensor_scalar(t1f[:], x2f[par][:], scT[:, t, 5:6], scT[:, t, 6:7],
                                           op0=ALU.mult, op1=ALU.add))
                    v.wait_ge(sV.h, sV.n)
                    sV.inc(v.tensor_tensor(t1f[:], t1f[:], gam_s[:], op=ALU.mult))
                    v.wait_ge(sV.h, sV.n)
                    sV.inc(v.tensor_tensor(xnn_sb[:, t, :], t1f[:], bet_s[:], op=ALU.add))
                    v.wait_ge(sV.h, sV.n)
                    sV.inc(v.bn_stats(bnF[:, 0:6], xnn_sb[:, t, 0:512]))
                    sV.inc(v.bn_stats(bnF[:, 6:12], xnn_sb[:, t, 512:1024]))
                    v.wait_ge(sV.h, sV.n)
                    sV.inc(v.bn_aggr(mvT[:, t, 2:4], bnF[:]))
                    v.wait_ge(sV.h, sV.n)
                    sV.inc(v.tensor_scalar(scT[:, t, 7:8], mvT[:, t, 2:3], mvT[:, t, 2:3], mvT[:, t, 3:4],
                                           op0=ALU.mult, op1=ALU.add))
                    v.wait_ge(sV.h, sV.n)
                    sV.inc(v.tensor_scalar(scT[:, t, 8:9], scT[:, t, 7:8], 1024.0, None, op0=ALU.mult))
                V(_)
                v_ss = sV.n
                ftrk[f"x2f{par}"] = v_ss

                def _(g, t=t, v_ss=v_ss):
                    g.wait_ge(sV.h, v_ss)
                    sG.inc(g.tensor_tensor(scT[:, t, 9:10], scT[:, t, 8:9], ph1_s[:], op=ALU.pow))
                G(_)
                fst[("n5", b, t)] = sG.n

            def fS7(b, t):
                par = t % 2
                w_n5 = fst[("n5", b, t)]
                w_xuT_old = fget("xuT", b, t - 2)

                def _(v, t=t, par=par, w_n5=w_n5, w_xuT_old=w_xuT_old):
                    v.wait_ge(sG.h, w_n5)
                    if w_xuT_old:
                        v.wait_ge(sP.h, w_xuT_old)
                    sV.inc(v.tensor_scalar(scT[:, t, 10:11], scT[:, t, 9:10], 1e-12, None, op0=ALU.max))
                    v.wait_ge(sV.h, sV.n)
                    sV.inc(v.reciprocal(scT[:, t, 11:12], scT[:, t, 10:11]))
                    v.wait_ge(sV.h, sV.n)
                    sV.inc(v.tensor_tensor(scT[:, t, 12:13], scT[:, t, 11:12], scl_s[:], op=ALU.mult))
                    v.wait_ge(sV.h, sV.n)
                    sV.inc(v.tensor_scalar(xu_tp[par][:], xnn_sb[:, t, :], scT[:, t, 12:13], None, op0=ALU.mult))
                V(_)
                v_xu = sV.n
                w_pb_free = fget("xufe", b, t - 1)

                def _(t_, t=t, par=par, v_xu=v_xu, w_pb_free=w_pb_free):
                    t_.wait_ge(sV.h, v_xu)
                    if w_pb_free:
                        t_.wait_ge(sA.h, w_pb_free)
                    for kc in range(8):
                        ii = t_.transpose(ptrbB[:, kc * 128:kc * 128 + 128], xu_tp[par][:, kc * 128:kc * 128 + 128],
                                          identb[:])
                    sP.inc(ii)
                T(_)
                fst[("xuT", b, t)] = sP.n
                p_xuT = sP.n
                w_lgmm_old = fget("lgmm", b, t - 2)

                def _(s, par=par, p_xuT=p_xuT, w_lgmm_old=w_lgmm_old):
                    s.wait_ge(sP.h, p_xuT)
                    if w_lgmm_old:
                        s.wait_ge(sP.h, w_lgmm_old)
                    sA.inc(s.copy(xufp[par][:], ptrbB[:]))
                A(_)
                fst[("xufe", b, t)] = sA.n

            def fS9(b, t):
                par = t % 2
                bank = pA if par == 0 else pB
                w_xuf = fst[("xufe", b, t)]
                w_bank = fget("lge", b, t - 2) or ftrk.get(f"pA{par}", 0)
                w_si0 = ftrk.get("si_done0", 0) if (b == 1 and t == par) else 0

                def _(t_, t=t, par=par, bank=bank, w_xuf=w_xuf, w_bank=w_bank, w_si0=w_si0):
                    t_.wait_ge(sA.h, w_xuf)
                    t_.wait_ge(sV.h, v_muu)
                    if w_bank:
                        t_.wait_ge(sA.h, w_bank)
                    if w_si0:
                        t_.wait_ge(sA.h, w_si0)
                    for kc in range(8):
                        t_.matmul(bank[:, 0:256], xufp[par][:, kc * 128:kc * 128 + 128],
                                  muu[:, kc * 256:kc * 256 + 256], start=(kc == 0), stop=(kc == 7),
                                  skip_group_check=True)
                    for sh in range(2):
                        for kc in range(8):
                            ii = t_.matmul(bank[:, 256 + sh * 128:384 + sh * 128],
                                           muu[:, kc * 256 + sh * 128:kc * 256 + sh * 128 + 128],
                                           xufp[par][:, kc * 128:kc * 128 + 128],
                                           start=(kc == 0), stop=(kc == 7), skip_group_check=True)
                    sP.inc(ii)
                T(_)
                fst[("lgmm", b, t)] = sP.n
                p_lg = sP.n
                w_nm_old = fget("nmc", b, t - 2)
                disp = ftrk.get("disp_done_v", 0)

                def _(s, t=t, par=par, bank=bank, p_lg=p_lg, w_nm_old=w_nm_old, disp=disp):
                    s.wait_ge(sP.h, p_lg)
                    if w_nm_old:
                        s.wait_ge(sV.h, w_nm_old)
                    sA.inc(s.copy(lg_tp[par][:], bank[:, 0:256]))
                    if disp:
                        s.wait_ge(sV.h, disp)
                    sA.inc(s.copy(lgsm[:, :, t * 128:t * 128 + 128],
                                  bank[:, 256:512].rearrange("p (c t) -> p c t", t=128)))
                A(_)
                fst[("lge", b, t)] = sA.n
                ftrk[f"pA{par}"] = sA.n

            def fS10a(b, t):
                par = t % 2
                w_lge = fst[("lge", b, t)]

                def _(v, t=t, par=par, w_lge=w_lge):
                    v.wait_ge(sA.h, w_lge)
                    sV.inc(v.tensor_reduce(scT[:, t, 13:14], lg_tp[par][:], axis=AX.X, op=ALU.max, negate=True))
                V(_)
                fst[("nmc", b, t)] = sV.n
                v_nmc = sV.n

                def _(s, t=t, par=par, v_nmc=v_nmc):
                    s.wait_ge(sV.h, v_nmc)
                    sA.inc(s.activation(expcp[par][:], lg_tp[par][:], AF.Exp, bias=scT[:, t, 13:14],
                                        accum_out=scT[:, t, 14:15]))
                A(_)
                fst[("expc", b, t)] = sA.n

            def fS10b(b, t):
                par = t % 2
                a_expc = fst[("expc", b, t)]

                def _(v, b=b, t=t, par=par, a_expc=a_expc):
                    v.wait_ge(sA.h, a_expc)
                    sV.inc(v.reciprocal(scT[:, t, 15:16], scT[:, t, 14:15]))
                    v.wait_ge(sV.h, sV.n)
                    sV.inc(v.tensor_scalar(comb[:, b, t, :], expcp[par][:], scT[:, t, 15:16], None, op0=ALU.mult))
                V(_)
                ftrk["comb"][b] = sV.n

            def emit_F1(b):
                for i in range(NT + 6):
                    if 0 <= i - 2 < NT:
                        fS5(b, i - 2)
                    if 0 <= i - 3 < NT:
                        fS7(b, i - 3)
                    if 0 <= i - 4 < NT:
                        fS9(b, i - 4)
                    if 0 <= i - 5 < NT:
                        fS10a(b, i - 5)
                    if 0 <= i - 6 < NT:
                        fS10b(b, i - 6)
                    if i < NT:
                        fS1(b, i)
                    if 0 <= i - 1 < NT:
                        fS3(b, i - 1)
                ftrk["lgsm_done"] = sA.n
                ftrk["xnn_done"] = sV.n
                if tuple() and dbg_names:
                    pass

                for c in range(2):
                    prev_dsp = ftrk.get("a_dsp_last", 0)

                    def _(v, c=c, prev_dsp=prev_dsp):
                        v.wait_ge(sA.h, ftrk["lgsm_done"])
                        if prev_dsp:
                            v.wait_ge(sA.h, prev_dsp)
                        sV.inc(v.tensor_reduce(scD[:, 0:1], lgsm[:, c, :], axis=AX.X, op=ALU.max, negate=True))
                    V(_)
                    v_nmd = sV.n

                    dspb_free = ftrk.get("dspb_free", 0)

                    def _(s, c=c, v_nmd=v_nmd, dspb_free=dspb_free):
                        s.wait_ge(sV.h, v_nmd)
                        if dspb_free:
                            s.wait_ge(sP.h, dspb_free)
                        sA.inc(s.activation(dspb[:], lgsm[:, c, :], AF.Exp, bias=scD[:, 0:1], accum_out=scD[:, 1:2]))
                        s.wait_ge(sA.h, sA.n)
                        sA.inc(s.activation(scD[:, 2:3], scD[:, 1:2], AF.Ln))
                    A(_)
                    a_lnz = sA.n

                    def _(v, a_lnz=a_lnz):
                        v.wait_ge(sA.h, a_lnz)
                        sV.inc(v.tensor_scalar(scD[:, 3:4], scD[:, 2:3], -1.0, scD[:, 0:1], op0=ALU.mult, op1=ALU.add))
                    V(_)
                    v_b2x = sV.n
                    ftrk["v_b2x"] = v_b2x

                    def _(s, c=c, v_b2x=v_b2x):
                        s.wait_ge(sV.h, v_b2x)
                        sA.inc(s.activation(dspb[:], lgsm[:, c, :], AF.Exp, bias=scD[:, 3:4]))
                    A(_)
                    v_dsp = sV.n
                    a_dsp = sA.n
                    ftrk["a_dsp_last"] = a_dsp

                    for rnd in range(2):
                        ptrb_free = ftrk["ptrb"]

                        def _(t, c=c, rnd=rnd, a_dsp=a_dsp, ptrb_free=ptrb_free):
                            t.wait_ge(sA.h, a_dsp)
                            if ptrb_free:
                                t.wait_ge(sA.h, ptrb_free)
                            for j in range(8):
                                tt = rnd * 8 + j
                                ii = t.transpose(ptrb[:, j * 128:j * 128 + 128],
                                                 dspb[:, tt * 128:tt * 128 + 128], identb[:])
                            sP.inc(ii)
                        T(_)
                        p_dT = sP.n
                        ftrk["dspb_free"] = p_dT
                        si_mm_done = ftrk.get("si_mm_done", 0)

                        def _(s, c=c, rnd=rnd, p_dT=p_dT, si_mm_done=si_mm_done):
                            s.wait_ge(sP.h, p_dT)
                            if si_mm_done:
                                s.wait_ge(sP.h, si_mm_done)
                            sA.inc(s.copy(dtm[:, rnd * 8:rnd * 8 + 8, c * 128:c * 128 + 128],
                                          ptrb[:].rearrange("p (j t) -> p j t", t=128)))
                        A(_)
                        ftrk["ptrb"] = sA.n
                v_dtm = sA.n
                ftrk["disp_done_v"] = sV.n
                if dbg_names:
                    dump(f"dtmA_b{b}", dtm[:, 0, :], sA, v_dtm)

                prev_si = 0
                for dc in range(8):
                    bank = pB if dc % 2 == 0 else pC

                    def _(t, b=b, dc=dc, bank=bank, v_dtm=v_dtm, prev_si=prev_si):
                        t.wait_ge(sA.h, v_dtm)
                        t.wait_ge(sV.h, ftrk["xnn_done"])
                        if prev_si:
                            t.wait_ge(sA.h, prev_si)
                        for tt in range(NT):
                            ii = t.matmul(bank[:, 0:256], xnn_sb[:, tt, dc * 128:dc * 128 + 128],
                                          dtm[:, tt, :], start=(tt == 0), stop=(tt == NT - 1))
                        sP.inc(ii)
                    T(_)
                    p_si = sP.n

                    def _(s, b=b, dc=dc, bank=bank, p_si=p_si):
                        s.wait_ge(sP.h, p_si)
                        sA.inc(s.copy(si_sb[:, b, dc, :], bank[:, 0:256]))
                    A(_)
                    prev_si = sA.n if dc >= 1 else 0
                    if dc >= 1:
                        prev_si = ftrk["si"].get(dc - 1, 0)
                    ftrk["si"][dc] = sA.n
                ftrk[f"si_done{b}"] = sA.n
                ftrk["si_mm_done"] = sP.n
                if dbg_names:
                    dump(f"dtmB_b{b}", dtm[:, 0, :], sP, sP.n)
                if dbg_names:
                    dump(f"xnn14_b_b{b}", xnn_sb[:, 14, :], sP, sP.n)
                if dbg_names:
                    dump(f"si_b{b}", si_sb[:, b], sA, sA.n)
                    dump(f"dtm_b{b}", dtm[:, 0, :], sA, sA.n)
                    dump(f"dtm7_b{b}", dtm[:, 7, :], sA, sA.n)
                    dump(f"dtm8_b{b}", dtm[:, 8, :], sA, sA.n)
                    dump(f"dtm15_b{b}", dtm[:, 15, :], sA, sA.n)

            def emit_wload(e):
                par = e % 4
                wfree = ftrk.get(f"wbuf{par}", 0)

                def _(g, e=e, par=par, wfree=wfree):
                    if wfree:
                        g.wait_ge(sP.h, wfree)
                    if e < 4:
                        g.wait_ge(sV.h, v_muu)
                    g.dma_start(out=w1buf[par][:], in_=w1_d[e]).then_inc(dE[par].h, 16)
                    g.dma_start(out=w2buf[par][:], in_=w2_d[e]).then_inc(dE[par].h, 16)
                    dE[par].n += 32
                G(_)
                ftrk[f"wload{e}"] = dE[par].n

            def emit_experts():
                si_done = max(ftrk["si_done0"], ftrk["si_done1"])
                for e in range(E):
                    par = e % 4
                    w_e = ftrk[f"wload{e}"]

                    pd_free = ftrk.get("pD_exp", 0)

                    def _(t, e=e, par=par, w_e=w_e, si_done=si_done, pd_free=pd_free):
                        t.wait_ge(dE[par].h, w_e)
                        t.wait_ge(sA.h, si_done)
                        if pd_free:
                            t.wait_ge(sA.h, pd_free)
                        for dc in range(8):
                            for hc in range(2):
                                ii = t.matmul(pD[:, hc * 32:hc * 32 + 32],
                                              w1buf[par][:, dc * 256 + hc * 128:dc * 256 + hc * 128 + 128],
                                              si_sb[:, :, dc, e * 16:e * 16 + 16],
                                              start=(dc == 0 and hc == 0), stop=(dc == 7 and hc == 1),
                                              skip_group_check=True)
                        sP.inc(ii)
                    T(_)
                    p_h = sP.n

                    def _(s, e=e, p_h=p_h):
                        s.wait_ge(sP.h, p_h)
                        sA.inc(s.activation(h_bf[:, 0:32], pD[:, 0:32], AF.Gelu, bias=b1f_s[:, 2 * e:2 * e + 1]))
                        sA.inc(s.activation(h_bf[:, 32:64], pD[:, 32:64], AF.Gelu, bias=b1f_s[:, 2 * e + 1:2 * e + 2]))
                    A(_)
                    a_h = sA.n
                    ftrk["pD_exp"] = a_h

                    pe_free = ftrk.get("pE_exp", 0)

                    def _(t, e=e, par=par, a_h=a_h, pe_free=pe_free):
                        t.wait_ge(sA.h, a_h)
                        if pe_free:
                            t.wait_ge(sA.h, pe_free)
                        for dh in range(2):
                            for hc in range(2):
                                ii = t.matmul(pE[0:32, dh * 512:dh * 512 + 512],
                                              h_bf[:, hc * 32:hc * 32 + 32],
                                              w2buf[par][:, hc * 1024 + dh * 512:hc * 1024 + dh * 512 + 512],
                                              start=(hc == 0), stop=(hc == 1), skip_group_check=True)
                        sP.inc(ii)
                    T(_)
                    p_so = sP.n
                    ftrk[f"wbuf{par}"] = p_so
                    if e + 4 < E:
                        emit_wload(e + 4)

                    spar = e % 2
                    stfree = ftrk.get(f"sost{spar}", 0)

                    def _(s, e=e, spar=spar, p_so=p_so, stfree=stfree):
                        s.wait_ge(sP.h, p_so)
                        if stfree:
                            s.wait_ge(dSh[spar].h, stfree)
                        sA.inc(s.copy(so_st[spar][0:32, 0:512], pE[0:32, 0:512]))
                        sA.inc(s.copy(so_st[spar][0:32, 512:1024], pE[0:32, 512:1024]))
                    A(_)
                    a_so = sA.n
                    ftrk["pE_exp"] = a_so

                    def _(g, e=e, spar=spar, a_so=a_so):
                        g.wait_ge(sA.h, a_so)
                        r0 = (e % 8) * 16
                        g.dma_start(out=so_stg[r0:r0 + 16, 0, e // 8, :], in_=so_st[spar][0:16, :]).then_inc(dSh[spar].h, 16)
                        g.dma_start(out=so_stg[r0:r0 + 16, 1, e // 8, :], in_=so_st[spar][16:32, :]).then_inc(dSh[spar].h, 16)
                        dSh[spar].n += 32
                    G(_)
                    ftrk[f"sost{spar}"] = dSh[spar].n

                def _(v):
                    v.wait_ge(dSh[0].h, dSh[0].n)
                    v.wait_ge(dSh[1].h, dSh[1].n)
                    for bb in range(2):
                        for c in range(2):
                            sV.inc(v.tensor_tensor(so_bf[:, bb, c, :], so_stg[:, bb, c, :],
                                                   b2r_s[:, c, :], op=ALU.add))
                V(_)
                ftrk["so_ready"] = sV.n
                if dbg_names:
                    dump("sobf_b0", so_bf[:, 0], sV, sV.n)
                    dump("sobf_b1", so_bf[:, 1], sV, sV.n)

            f3 = {}

            def f3A(b, t):
                par = t % 2
                xfree = max(ftrk.get(f"x2r{par}", 0), ftrk.get(f"x2f{par}", 0))

                def _(q, b=b, t=t, par=par, xfree=xfree):
                    if xfree:
                        q.wait_ge(sV.h, xfree)
                    dXp[par].inc(q.dma_start(out=x2f[par][:], in_=x2_tile_v[b, t]), 16)
                Q(_)
                f3[("xld", b, t)] = dXp[par].n

                pt = ptrb if par == 0 else ptrbB
                pt_free = f3.get(("cbe", b, t - 2), 0)
                if not pt_free:
                    pt_free = ftrk["ptrb"] if par == 0 else fst.get(("xufe", 1, NT - 1), 0)

                def _(t_, b=b, t=t, pt=pt, pt_free=pt_free):
                    t_.wait_ge(sV.h, ftrk["comb"][b])
                    if pt_free:
                        t_.wait_ge(sA.h, pt_free)
                    t_.transpose(pt[:, 0:128], comb[:, b, t, 0:128], identb[:])
                    sP.inc(t_.transpose(pt[:, 128:256], comb[:, b, t, 128:256], identb[:]))
                T(_)
                p_cT = sP.n

                def _(s, par=par, pt=pt, p_cT=p_cT):
                    s.wait_ge(sP.h, p_cT)
                    sA.inc(s.copy(cb_smp[par][:], pt[:, 0:256]))
                A(_)
                f3[("cbe", b, t)] = sA.n

            def f3B(b, t):
                par = t % 2
                banks = (pB, pC) if par == 0 else (pD, pE[:, 0:512])
                a_cb = f3[("cbe", b, t)]
                bfree = f3.get(("apply", b, t - 2), 0)

                def _(t_, b=b, par=par, banks=banks, a_cb=a_cb, bfree=bfree):
                    t_.wait_ge(sA.h, a_cb)
                    t_.wait_ge(sV.h, ftrk["so_ready"])
                    if bfree:
                        t_.wait_ge(sV.h, bfree)
                    for dh in range(2):
                        for c in range(2):
                            ii = t_.matmul(banks[dh][:, 0:512], cb_smp[par][:, c * 128:c * 128 + 128],
                                           so_bf[:, b, c, dh * 512:dh * 512 + 512],
                                           start=(c == 0), stop=(c == 1), skip_group_check=True)
                    sP.inc(ii)
                T(_)
                p_o = sP.n
                f3[("omm", b, t)] = p_o

                def _(v, t=t, banks=banks, p_o=p_o):
                    v.wait_ge(sP.h, p_o)
                    sV.inc(v.bn_stats(bnF[:, 0:6], banks[0][:, 0:512]))
                    sV.inc(v.bn_stats(bnF[:, 6:12], banks[1][:, 0:512]))
                    v.wait_ge(sV.h, sV.n)
                    sV.inc(v.bn_aggr(mvT[:, t, 0:2], bnF[:]))
                    v.wait_ge(sV.h, sV.n)
                    sV.inc(v.tensor_scalar(scT[:, t, 0:1], mvT[:, t, 1:2], EPS, None, op0=ALU.add))
                V(_)
                v_u6 = sV.n

                def _(g, t=t, v_u6=v_u6):
                    g.wait_ge(sV.h, v_u6)
                    sG.inc(g.tensor_tensor(scT[:, t, 1:2], scT[:, t, 0:1], nh1_s[:], op=ALU.pow))
                G(_)
                f3[("r6", b, t)] = sG.n

            def f3C(b, t):
                par = t % 2
                banks = (pB, pC) if par == 0 else (pD, pE[:, 0:512])
                g_r6 = f3[("r6", b, t)]
                x_ld = f3[("xld", b, t)]
                ofree = ftrk.get(f"ostore{par}", 0)

                def _(v, b=b, t=t, par=par, banks=banks, g_r6=g_r6, x_ld=x_ld, ofree=ofree):
                    v.wait_ge(sG.h, g_r6)
                    v.wait_ge(dXp[par].h, x_ld)
                    if ofree:
                        v.wait_ge(dOp[par].h, ofree)
                    sV.inc(v.tensor_scalar(scT[:, t, 2:3], mvT[:, t, 0:1], scT[:, t, 1:2], -1.0,
                                           op0=ALU.mult, op1=ALU.mult))
                    v.wait_ge(sV.h, sV.n)
                    sV.inc(v.tensor_scalar(ln5_t[:, 0:512], banks[0][:, 0:512], scT[:, t, 1:2], scT[:, t, 2:3],
                                           op0=ALU.mult, op1=ALU.add))
                    sV.inc(v.tensor_scalar(ln5_t[:, 512:1024], banks[1][:, 0:512], scT[:, t, 1:2], scT[:, t, 2:3],
                                           op0=ALU.mult, op1=ALU.add))
                    v.wait_ge(sV.h, sV.n)
                    sV.inc(v.tensor_tensor(out_t[par][:], x2f[par][:], ln5_t[:], op=ALU.add))
                V(_)
                v_out = sV.n
                f3[("apply", b, t)] = v_out
                ftrk[f"x2r{par}"] = v_out

                def _(q, b=b, t=t, par=par, v_out=v_out):
                    q.wait_ge(sV.h, v_out)
                    dOp[par].inc(q.dma_start(out=out_tile_v[b, t], in_=out_t[par][:]), 16)
                Q(_)
                ftrk[f"ostore{par}"] = dOp[par].n

            def emit_F3(b):
                if b == 1:
                    f3[("cbe", 1, -2)] = f3[("cbe", 0, NT - 2)]
                    f3[("cbe", 1, -1)] = f3[("cbe", 0, NT - 1)]
                    f3[("apply", 1, -2)] = f3[("apply", 0, NT - 2)]
                    f3[("apply", 1, -1)] = f3[("apply", 0, NT - 1)]
                for i in range(NT + 2):
                    if 0 <= i - 2 < NT:
                        f3C(b, i - 2)
                    if 0 <= i - 1 < NT:
                        f3B(b, i - 1)
                    if i < NT:
                        f3A(b, i)

            for _e in range(4):
                emit_wload(_e)
            emit_F1(0)
            emit_F1(1)
            esFd.close()
            cb_smp = [esF.enter_context(nc.sbuf_tensor(f"cb_sm{i}", [P, 256], BF16)) for i in range(2)]
            ln5_t = esF.enter_context(nc.sbuf_tensor("ln5_t", [P, D], F32))
            out_t = [esF.enter_context(nc.sbuf_tensor(f"outt{i}", [P, D], F32)) for i in range(2)]
            emit_experts()
            emit_F3(0)
            emit_F3(1)
            esF.close()

        x2_done = (dSp[0].n, dSp[1].n)

        def _(q):
            if x2_done[0]:
                q.wait_ge(dSp[0].h, x2_done[0])
            if x2_done[1]:
                q.wait_ge(dSp[1].h, x2_done[1])
            if "B" in phases and "F" not in phases:
                for b in range(2):
                    dO.inc(q.dma_start(out=out_d[b], in_=x2scr[b]), 16)
            if dO.n:
                q.wait_ge(dO.h, dO.n)
            if dOp[0].n:
                q.wait_ge(dOp[0].h, dOp[0].n)
            if dOp[1].n:
                q.wait_ge(dOp[1].h, dOp[1].n)
            if dDbg.n:
                q.wait_ge(dDbg.h, dDbg.n)
        Q(_)

    f = nc.m.functions[0]
    blocks = f.blocks
    if len(blocks) > 2:
        def ekey(bb):
            engs = {str(getattr(ins, "engine", None)) for ins in bb.instructions}
            return sorted(engs)[0] if engs else "zz"
        f.blocks = [blocks[0]] + sorted(blocks[1:-1], key=ekey) + [blocks[-1]]
    return nc, dbg_outs


def _host_prep(inp):
    f = np.float32
    Wq, Wk, Wv, Wo = (np.asarray(inp[k], f) for k in ("L0_attn_Wq", "L0_attn_Wk", "L0_attn_Wv", "L0_attn_Wo"))

    def wlay(W):
        return np.ascontiguousarray(
            W.T.reshape(8, 128, 8, 128).transpose(1, 0, 2, 3).reshape(128, 8192))

    planes_T = np.asarray(inp["L0_attn_planes_T"], f)
    negA = np.zeros((64, 1024), f)
    for j in range(64):
        negA[j, j * 16:(j + 1) * 16] = -1.0
    negB = np.zeros((8, 1024), f)
    for j in range(8):
        negB[j, j * 128:(j + 1) * 128] = -1.0
    mu = np.asarray(inp["L0_ffn_mu"], f).reshape(1024, 256)
    w1 = np.asarray(inp["L0_ffn_w1"], f)
    w2 = np.asarray(inp["L0_ffn_w2"], f)
    b1 = np.asarray(inp["L0_ffn_b1"], f)
    com = {
        "wq": wlay(Wq), "wk": wlay(Wk), "wv": wlay(Wv), "wo": wlay(Wo),
        "bo": np.asarray(inp["L0_attn_bo"], f).reshape(1, 1024),
        "planes2": np.concatenate([planes_T, planes_T], 0),
        "protos": np.asarray(inp["L0_attn_protos_T"], f),
        "ltemp": np.asarray(inp["L0_attn_logit_temp"], f).reshape(1, 1),
        "negindA": negA, "negindB": negB,
        "ident": np.eye(128, dtype=f),
        "ones1": np.ones((1, 128), f),
        "onescol": np.ones((128, 1), f),
        "nh8": np.full((128, 8), -0.5, f),
        "nh1": np.full((128, 1), -0.5, f),
        "ph1": np.full((128, 1), 0.5, f),
        "mul": np.ascontiguousarray(mu.reshape(8, 128, 256).transpose(1, 0, 2).reshape(128, 2048)),
        "gam": np.ascontiguousarray(np.broadcast_to(np.asarray(inp["L0_ffn_ln_gamma"], f), (128, 1024))),
        "bet": np.ascontiguousarray(np.broadcast_to(np.asarray(inp["L0_ffn_ln_beta"], f), (128, 1024))),
        "scl": np.ascontiguousarray(np.broadcast_to(np.asarray(inp["L0_ffn_scale"], f).reshape(1, 1), (128, 1))),
        "w1l": np.ascontiguousarray(w1.reshape(E, 8, 128, 2, 128).transpose(0, 2, 1, 3, 4).reshape(E, 128, 2048)),
        "w2l": np.ascontiguousarray(w2.reshape(E, 2, 128, 2, 512).transpose(0, 2, 1, 3, 4).reshape(E, 128, 2048)),
        "b1f": np.ascontiguousarray(b1.reshape(16, 2, 128).transpose(2, 0, 1).reshape(128, 32)),
        "b2r": np.ascontiguousarray(np.repeat(np.asarray(inp["L0_ffn_b2"], f), 16, axis=0)),
    }
    return com


def kernel(**inputs):
    x = np.ascontiguousarray(np.asarray(inputs["L0_attn_x"], np.float32))
    com = _host_prep(inputs)
    nc, _ = build_program(DBG_NAMES)
    in_maps = []
    for c in range(8):
        m = dict(com)
        m["x"] = np.ascontiguousarray(x[2 * c:2 * c + 2])
        in_maps.append(m)
    res = run_bass_kernel_spmd(nc, in_maps, core_ids=list(range(8)))
    return np.concatenate([r["out"] for r in res.results], axis=0)
